# version 01
# baseline (speedup 1.0000x reference)
import numpy as np
import ml_dtypes
import concourse.bass as bass
import concourse.mybir as mybir
from concourse.bass_utils import run_bass_kernel_spmd

F32 = mybir.dt.float32
BF16 = mybir.dt.bfloat16
AF = mybir.ActivationFunctionType
ALU = mybir.AluOpType
AX = mybir.AxisListType

B, S, D, H, DH, NL = 16, 2048, 1024, 8, 128, 4
FI = 2048
FO = 2816
EPS = 1e-5
NB = 2
NT = 16
HALF = 1024
QCW = 512

PERM = np.concatenate([np.arange(0, 128, 2), np.arange(1, 128, 2)])


def host_consts():
    c = {}
    gam = 1.0 - np.exp(np.linspace(np.log(1 / 32), np.log(1 / 512), H)).astype(np.float64)
    invf = 1.0 / 10000.0 ** (np.arange(0, DH, 2, dtype=np.float64) / DH)
    n = np.arange(S, dtype=np.float64)
    ang = n[:, None] * invf[None, :]
    cos, sin = np.cos(ang), np.sin(ang)
    CQ = np.concatenate([cos.T, cos.T], 0)
    SQ = np.concatenate([-sin.T, sin.T], 0)
    c["cq"] = CQ.astype(ml_dtypes.bfloat16)
    c["sq"] = SQ.astype(ml_dtypes.bfloat16)
    ckt = np.zeros((H, 128, S), dtype=np.float64)
    skt = np.zeros((H, 128, S), dtype=np.float64)
    SKbase = np.concatenate([sin.T, -sin.T], 0)
    for h in range(H):
        gn = gam[h] ** (-n)
        ckt[h] = CQ * gn[None, :]
        skt[h] = SKbase * gn[None, :]
    c["ckt"] = ckt.astype(ml_dtypes.bfloat16)
    c["skt"] = skt.astype(ml_dtypes.bfloat16)
    dl = np.arange(512, dtype=np.float64)
    e2 = np.stack([EPS * gam[h] ** (-2.0 * dl) for h in range(H)], 0)
    c["e2t"] = np.broadcast_to(e2[:, None, :], (H, 128, 512)).astype(np.float32).copy()
    c["qseg"] = [[float(gam[h] ** (512.0 * t)) for t in range(4)] for h in range(H)]
    i_ = np.arange(128)[:, None]
    t_ = np.arange(896)[None, :]
    c["bigmask"] = (t_ >= i_ + 384).astype(ml_dtypes.bfloat16)
    c["ones128"] = np.full((128, 128), 1.0 / 128.0, dtype=ml_dtypes.bfloat16)
    c["onesrow"] = np.ones((1, 128), dtype=ml_dtypes.bfloat16)
    pm = np.zeros((128, 128), dtype=np.float32)
    pm[np.arange(128), (np.arange(128) + 64) % 128] = 1.0
    c["pswap"] = pm.astype(ml_dtypes.bfloat16)
    return c


class Sem:
    def __init__(self, h):
        self.h, self.n = h, 0


class Prog:
    def __init__(self):
        self.streams = {e: [] for e in ("sync", "tensor", "vector", "scalar", "gpsimd")}

    def emit(self, eng, fn):
        self.streams[eng].append(fn)


def build_program(cfg):
    n_layers = cfg.get("n_layers", NL)
    do_attn = cfg.get("do_attn", True)
    do_ffn = cfg.get("do_ffn", True)
    n_batches = cfg.get("n_batches", NB)
    tap_points = cfg.get("taps", {})
    qseg = cfg["qseg"]

    nc = bass.Bass("TRN2", target_bir_lowering=False)

    dx = nc.dram_tensor("x", [NB, S, D], F32, kind="ExternalInput").ap()
    dwa = [nc.dram_tensor(nm, [NL, D, D], F32, kind="ExternalInput").ap()
           for nm in ("wq", "wk", "wv", "wg", "wo")]
    dw1 = nc.dram_tensor("w1", [NL, D, FI], F32, kind="ExternalInput").ap()
    dw2 = nc.dram_tensor("w2", [NL, FI, D], F32, kind="ExternalInput").ap()
    dlng = nc.dram_tensor("lng", [128, 8], F32, kind="ExternalInput").ap()
    dlnb = nc.dram_tensor("lnb", [128, 8], F32, kind="ExternalInput").ap()
    dgn = nc.dram_tensor("gn", [128, 2 * NL * H], F32, kind="ExternalInput").ap()
    dfg = nc.dram_tensor("wfg", [FO, D], F32, kind="ExternalInput").ap()
    dfu = nc.dram_tensor("wfu", [FO, D], F32, kind="ExternalInput").ap()
    dfd = nc.dram_tensor("wfd", [D, FO], F32, kind="ExternalInput").ap()
    dcq = nc.dram_tensor("cq", [128, S], BF16, kind="ExternalInput").ap()
    dsq = nc.dram_tensor("sq", [128, S], BF16, kind="ExternalInput").ap()
    dck = nc.dram_tensor("ckt", [H, 128, S], BF16, kind="ExternalInput").ap()
    dsk = nc.dram_tensor("skt", [H, 128, S], BF16, kind="ExternalInput").ap()
    de2 = nc.dram_tensor("e2t", [H, 128, 512], F32, kind="ExternalInput").ap()
    dbm = nc.dram_tensor("bigmask", [128, 896], BF16, kind="ExternalInput").ap()
    don = nc.dram_tensor("ones128", [128, 128], BF16, kind="ExternalInput").ap()
    dor = nc.dram_tensor("onesrow", [1, 128], BF16, kind="ExternalInput").ap()
    dpm = nc.dram_tensor("pswap", [128, 128], BF16, kind="ExternalInput").ap()
    dout = nc.dram_tensor("out", [NB, S, D], F32, kind="ExternalOutput").ap()

    wsp_att = nc.dram_tensor("wsp_att", [NL, 5, D, D], BF16, kind="Internal").ap()
    wsp_w1 = nc.dram_tensor("wsp_w1", [NL, D, FI], BF16, kind="Internal").ap()
    wsp_w2 = nc.dram_tensor("wsp_w2", [NL, FI, D], BF16, kind="Internal").ap()
    wsp_fgu = nc.dram_tensor("wsp_fgu", [2, 128, 8 * FO], BF16, kind="Internal").ap()
    wsp_fd = nc.dram_tensor("wsp_fd", [128, 22 * D], BF16, kind="Internal").ap()
    x2_dram = nc.dram_tensor("x2s", [S, D], F32, kind="Internal").ap()
    rv_dram = nc.dram_tensor("rvs", [NL, 1024], BF16, kind="Internal").ap()
    hq_dram = nc.dram_tensor("hqs", [128, 22 * S], BF16, kind="Internal").ap()

    dtaps = {}
    for pname, (shape, dt) in tap_points.items():
        dtaps[pname] = nc.dram_tensor("tap_" + pname, shape, dt, kind="ExternalOutput").ap()

    from contextlib import ExitStack
    ctx = ExitStack()
    E = ctx.enter_context

    R = E(nc.sbuf_tensor("R", [128, 16384], BF16))
    KT = E(nc.sbuf_tensor("KT", [128, 16384], BF16))
    VB = E(nc.sbuf_tensor("VB", [128, 16384], BF16))
    QGH = E(nc.sbuf_tensor("QGH", [128, 16384], BF16))
    XNT = E(nc.sbuf_tensor("XNT", [128, 8192], BF16))
    QT = QGH[:, 0:8192]
    GT = QGH[:, 8192:16384]
    H1T = QGH[:, 0:16384]
    X2NT = XNT[:, 0:8192]
    XQT = KT[:, 0:16384]
    FDW = VB[:, 0:11264]
    QQ = VB[:, 11264 + 0:11264 + 2816]
    FGW = [QGH[:, 0:4096], QGH[:, 4096:8192]]
    FUW = [QGH[:, 8192:12288], QGH[:, 12288:16384]]
    FHQ = [XNT[:, 0:2816], XNT[:, 2816:5632]]

    CKS = E(nc.sbuf_tensor("CKS", [128, 2, 512], BF16))
    SKS = E(nc.sbuf_tensor("SKS", [128, 2, 512], BF16))
    E2S = E(nc.sbuf_tensor("E2S", [128, 2, 512], F32))
    BM = E(nc.sbuf_tensor("BM", [128, 896], BF16))
    ON1 = E(nc.sbuf_tensor("ON1", [128, 128], BF16))
    ONR = E(nc.sbuf_tensor("ONR", [1, 128], BF16))
    PSW = E(nc.sbuf_tensor("PSW", [128, 128], BF16))
    GCOL = E(nc.sbuf_tensor("GCOL", [128, 8], F32))
    BCOL = E(nc.sbuf_tensor("BCOL", [128, 8], F32))
    BCOLB = E(nc.sbuf_tensor("BCOLB", [128, 8], BF16))
    GNC = E(nc.sbuf_tensor("GNC", [128, 2 * NL * H], F32))
    RQCL = E(nc.sbuf_tensor("RQCL", [128, NL * 8], F32))
    RKCL = E(nc.sbuf_tensor("RKCL", [128, NL * 8], F32))
    RGCL = E(nc.sbuf_tensor("RGCL", [128, NL * 8], F32))
    RW1CL = E(nc.sbuf_tensor("RW1CL", [128, NL * 16], F32))
    RVR1 = E(nc.sbuf_tensor("RVR1", [1, 1024], BF16))
    RQS = E(nc.sbuf_tensor("RQS", [128, 1], F32))
    WF32 = E(nc.sbuf_tensor("WF32", [128, 2, 1024], F32))
    WPL = E(nc.sbuf_tensor("WPL", [128, 1024], BF16))
    WSC = E(nc.sbuf_tensor("WSC", [128, 2, 1024], BF16))
    WKV = E(nc.sbuf_tensor("WKV", [128, 2, 1024], BF16))
    WMV = E(nc.sbuf_tensor("WMV", [128, 2, 1024], BF16))
    XT32 = E(nc.sbuf_tensor("XT32", [128, 2, 1024], F32))
    ZROLL = E(nc.sbuf_tensor("ZROLL", [128, 1024], BF16))
    STATS = E(nc.sbuf_tensor("STATS", [128, NT * 12], F32))
    MU = E(nc.sbuf_tensor("MU", [128, NT], F32))
    VAR = E(nc.sbuf_tensor("VAR", [128, NT], F32))
    RSTD = E(nc.sbuf_tensor("RSTD", [128, NT], F32))
    NMR = E(nc.sbuf_tensor("NMR", [128, NT], F32))
    AGG2 = E(nc.sbuf_tensor("AGG2", [128, 2], F32))
    ATS = E(nc.sbuf_tensor("ATS", [128, 2, 512], BF16))
    YSB = E(nc.sbuf_tensor("YSB", [128, 512], BF16))
    Y2B = E(nc.sbuf_tensor("Y2B", [128, 512], BF16))
    MSQ = E(nc.sbuf_tensor("MSQ", [128, 512], BF16))
    UB = E(nc.sbuf_tensor("UB", [128, 512], BF16))
    NMU = E(nc.sbuf_tensor("NMU", [128, 512], F32))
    CAPC = E(nc.sbuf_tensor("CAPC", [128, 2], F32))
    QM = E(nc.sbuf_tensor("QM", [128, 32], F32))
    QSH = E(nc.sbuf_tensor("QSH", [128, 32], F32))
    QSC = E(nc.sbuf_tensor("QSC", [128, 32], F32))
    QU = E(nc.sbuf_tensor("QU", [128, 32], F32))
    QY = E(nc.sbuf_tensor("QY", [128, QCW], F32))
    QT1 = E(nc.sbuf_tensor("QT1", [128, QCW], F32))
    QT2 = E(nc.sbuf_tensor("QT2", [128, QCW], F32))
    QSG = E(nc.sbuf_tensor("QSG", [128, QCW], F32))
    FQR = XT32
    FH = NMU
    FSIL = QT2
    TRT = ZROLL

    PS = [E(nc.psum_tensor("ps%d" % i, [128, 512], F32)) for i in range(8)]

    sems = {}
    for nm in ("dW", "dG", "sV", "sS", "sT"):
        sems[nm] = Sem(E(nc.semaphore(nm)))
    dW, dG, sV, sS, sT = (sems[k] for k in ("dW", "dG", "sV", "sS", "sT"))

    P = Prog()

    def w(eng, sem, val=None):
        v = sem.n if val is None else val
        P.emit(eng, lambda e, sem=sem, v=v: e.wait_ge(sem.h, v))

    def join(producer_sems, consumer_engs):
        for eng in consumer_engs:
            for sem in producer_sems:
                w(eng, sem)

    def op(eng, sem, fn, k=1):
        sem.n += k
        P.emit(eng, lambda e, fn=fn, sem=sem, k=k: fn(e).then_inc(sem.h, k))

    def opn(eng, fn):
        P.emit(eng, fn)

    def dma(eng_name, sem, dst, src, transpose=False):
        if transpose:
            op(eng_name, sem, lambda e, d=dst, s=src: e.dma_start_transpose(d, s), k=16)
        else:
            op(eng_name, sem, lambda e, d=dst, s=src: e.dma_start(d, s), k=16)

    def barrier():
        join((dW, dG, sV, sS, sT), ("sync", "tensor", "vector", "scalar", "gpsimd"))

    def tap(pname, buf_ap):
        if pname in dtaps:
            barrier()
            dma("gpsimd", dG, dtaps[pname], buf_ap)
            barrier()

    nc.const_aps.register(F32, float(EPS), CAPC[:, 0:1])
    nc.const_aps.register(F32, -0.5, CAPC[:, 1:2])

    for dst, src in ((BM[:], dbm), (ON1[:], don), (ONR[:], dor), (PSW[:], dpm),
                     (GCOL[:], dlng), (BCOL[:], dlnb), (GNC[:], dgn)):
        dma("sync", dW, dst, src)
    join((dW,), ("vector",))
    op("vector", sV, lambda e: e.tensor_copy(BCOLB[:], BCOL[:]))
    def f(e):
        e.memset(CAPC[:, 0:1], float(EPS))
        return e.memset(CAPC[:, 1:2], -0.5)
    op("vector", sV, f)
    join((sV,), ("scalar", "tensor", "gpsimd", "sync"))

    def emit_wprep_layer(l):
        for wi in range(5):
            folded = wi < 4
            for c in range(8):
                buf = c % 2
                join((sV,) + ((sS,) if folded else ()), ("sync",))
                dma("sync", dW, WF32[:, buf], dwa[wi][l, c * 128:(c + 1) * 128, :])
                if folded:
                    join((dW,), ("scalar",))
                    if c > 0:
                        join((sT,), ("scalar",))
                    op("scalar", sS, lambda e, buf=buf: e.copy(WPL[:], WF32[:, buf]))
                    join((sS,), ("tensor",))
                    if wi == 2:
                        def f(e, c=c):
                            e.matmul(PS[0][0:1, :], BCOLB[:, c:c + 1], WPL[:, 0:512],
                                     start=(c == 0), stop=(c == 7))
                            return e.matmul(PS[1][0:1, :], BCOLB[:, c:c + 1], WPL[:, 512:1024],
                                            start=(c == 0), stop=(c == 7))
                        op("tensor", sT, f)
                    else:
                        def f(e, c=c):
                            ins = None
                            for h in range(H):
                                ins = e.matmul(PS[h][:, 0:1], WPL[:, h * 128:(h + 1) * 128],
                                               BCOLB[:, c:c + 1], start=(c == 0), stop=(c == 7))
                            return ins
                        op("tensor", sT, f)
                    join((dW,), ("vector",))
                    op("vector", sV, lambda e, buf=buf, c=c: e.tensor_scalar(
                        WSC[:, buf], WF32[:, buf], GCOL[:, c:c + 1], None, ALU.mult))
                else:
                    join((dW,), ("vector",))
                    op("vector", sV, lambda e, buf=buf: e.tensor_copy(WSC[:, buf], WF32[:, buf]))
                join((sV,), ("gpsimd",))
                if c >= 2:
                    join((dG,), ("vector",))
                dma("gpsimd", dG, wsp_att[l, wi, c * 128:(c + 1) * 128, :], WSC[:, buf])
            if folded:
                join((sT,), ("vector",))
                if wi == 2:
                    def f(e):
                        e.tensor_copy(RVR1[0:1, 0:512], PS[0][0:1, :])
                        return e.tensor_copy(RVR1[0:1, 512:1024], PS[1][0:1, :])
                    op("vector", sV, f)
                    join((sV,), ("gpsimd",))
                    dma("gpsimd", dG, rv_dram[l:l + 1, :], RVR1[0:1, :])
                else:
                    dstc = {0: RQCL, 1: RKCL, 3: RGCL}[wi]
                    def f(e, dstc=dstc, l=l):
                        ins = None
                        for h in range(H):
                            ins = e.tensor_copy(dstc[:, l * 8 + h:l * 8 + h + 1], PS[h][:, 0:1])
                        return ins
                    op("vector", sV, f)
                join((sV,), ("tensor",))
        for ch in range(2):
            for c in range(8):
                buf = c % 2
                join((sV, sS), ("sync",))
                dma("sync", dW, WF32[:, buf], dw1[l, c * 128:(c + 1) * 128, ch * 1024:(ch + 1) * 1024])
                join((dW,), ("scalar",))
                if c > 0 or ch > 0:
                    join((sT,), ("scalar",))
                op("scalar", sS, lambda e, buf=buf: e.copy(WPL[:], WF32[:, buf]))
                join((sS,), ("tensor",))
                def f(e, c=c):
                    ins = None
                    for h in range(H):
                        ins = e.matmul(PS[h][:, 0:1], WPL[:, h * 128:(h + 1) * 128],
                                       BCOLB[:, c:c + 1], start=(c == 0), stop=(c == 7))
                    return ins
                op("tensor", sT, f)
                join((dW,), ("vector",))
                op("vector", sV, lambda e, buf=buf, c=c: e.tensor_scalar(
                    WSC[:, buf], WF32[:, buf], GCOL[:, c:c + 1], None, ALU.mult))
                join((sV,), ("gpsimd",))
                if c >= 2 or ch > 0:
                    join((dG,), ("vector",))
                dma("gpsimd", dG, wsp_w1[l, c * 128:(c + 1) * 128, ch * 1024:(ch + 1) * 1024], WSC[:, buf])
            join((sT,), ("vector",))
            def f(e, l=l, ch=ch):
                ins = None
                for h in range(H):
                    ins = e.tensor_copy(RW1CL[:, l * 16 + ch * 8 + h:l * 16 + ch * 8 + h + 1],
                                        PS[h][:, 0:1])
                return ins
            op("vector", sV, f)
            join((sV,), ("tensor",))
        for c in range(16):
            buf = c % 2
            join((sV,), ("sync",))
            join((dG,), ("vector",))
            dma("sync", dW, WF32[:, buf], dw2[l, c * 128:(c + 1) * 128, :])
            join((dW,), ("vector",))
            op("vector", sV, lambda e, buf=buf: e.tensor_copy(WSC[:, buf], WF32[:, buf]))
            join((sV,), ("gpsimd",))
            dma("gpsimd", dG, wsp_w2[l, c * 128:(c + 1) * 128, :], WSC[:, buf])
        barrier()

    def ln_stats_tile(src_ap, nt):
        def f(e, src_ap=src_ap, nt=nt):
            st = STATS[:, nt * 12:(nt + 1) * 12]
            e.bn_stats(st[:, 0:6], src_ap[:, 0:512])
            e.bn_stats(st[:, 6:12], src_ap[:, 512:1024])
            e.drain()
            e.bn_aggr(AGG2[:], st)
            e.drain()
            e.tensor_copy(MU[:, nt:nt + 1], AGG2[:, 0:1])
            return e.tensor_copy(VAR[:, nt:nt + 1], AGG2[:, 1:2])
        op("vector", sV, f)

    def ln_finish(tiles):
        lo, hi = min(tiles), max(tiles) + 1
        join((sV,), ("scalar",))
        def f(e, lo=lo, hi=hi):
            e.activation(RSTD[:, lo:hi], VAR[:, lo:hi], AF.Ln, bias=float(EPS))
            return e.activation(RSTD[:, lo:hi], RSTD[:, lo:hi], AF.Exp, scale=-0.5)
        op("scalar", sS, f)
        join((sS,), ("vector",))
        op("vector", sV, lambda e, lo=lo, hi=hi: e.scalar_tensor_tensor(
            NMR[:, lo:hi], MU[:, lo:hi], -1.0, RSTD[:, lo:hi], ALU.mult, ALU.mult))
        join((sV,), ("scalar",))

    def ln_affine(src_ap, dst_ap, nt):
        op("scalar", sS, lambda e, src_ap=src_ap, dst_ap=dst_ap, nt=nt: e.activation(
            dst_ap, src_ap, AF.Identity, bias=NMR[:, nt:nt + 1], scale=RSTD[:, nt:nt + 1]))

    def r_tile(nt):
        return R[:, nt * 1024:(nt + 1) * 1024]

    def emit_top_ln(b):
        for nt in range(NT):
            buf = nt % 2
            join((sV, sS), ("sync",))
            dma("sync", dW, XT32[:, buf], dx[b, nt * 128:(nt + 1) * 128, :])
            join((dW,), ("vector",))
            ln_stats_tile(XT32[:, buf], nt)
            ln_finish([nt])
            ln_affine(XT32[:, buf], r_tile(nt), nt)

    def emit_xnt_half(half):
        for j in range(8):
            nt = half * 8 + j
            if j > 0:
                join((dG,), ("scalar",))
            ln_affine(r_tile(nt), ZROLL[:], nt)
            join((sS,), ("scalar",))
            for c in range(8):
                dma("scalar", dG, XNT[:, c * HALF + j * 128:c * HALF + (j + 1) * 128],
                    ZROLL[:, c * 128:(c + 1) * 128], transpose=True)

    def emit_k_half(l, half):
        for segc in range(2):
            sg = 2 * half + segc
            for c in range(8):
                wb = c % 2
                join((sT,), ("sync",))
                dma("sync", dW, WKV[:, wb], wsp_att[l, 1, c * 128:(c + 1) * 128, :])
                join((dW,), ("tensor",))
                def f(e, c=c, wb=wb, segc=segc):
                    ins = None
                    for h in range(H):
                        ins = e.matmul(PS[h][:], WKV[:, wb][:, h * 128:(h + 1) * 128],
                                       XNT[:, c * HALF + segc * 512:c * HALF + (segc + 1) * 512],
                                       start=(c == 0), stop=(c == 7))
                    return ins
                op("tensor", sT, f)
            for h in range(H):
                tb = h % 2
                dma("sync", dW, CKS[:, tb], dck[h, :, sg * 512:(sg + 1) * 512])
                dma("sync", dW, SKS[:, tb], dsk[h, :, sg * 512:(sg + 1) * 512])
                join((sT,), ("scalar",))
                if h > 0:
                    join((sT,), ("scalar",))
                op("scalar", sS, lambda e, h=h, l=l: e.activation(
                    ZROLL[:, 0:512], PS[h][:], AF.Identity, bias=RKCL[:, l * 8 + h:l * 8 + h + 1]))
                join((sS,), ("tensor",))
                op("tensor", sT, lambda e, h=h: e.matmul(
                    PS[h][:], PSW[:], ZROLL[:, 0:512], start=True, stop=True))
                join((sT, dW), ("vector",))
                def f(e, h=h, sg=sg, tb=tb):
                    e.tensor_tensor(ATS[:, 0], PS[h][:], SKS[:, tb], ALU.mult)
                    e.tensor_tensor(ATS[:, 1], ZROLL[:, 0:512], CKS[:, tb], ALU.mult)
                    return e.tensor_add(KT[:, h * S + sg * 512:h * S + (sg + 1) * 512],
                                        ATS[:, 0], ATS[:, 1])
                op("vector", sV, f)
                join((sV,), ("tensor", "scalar", "sync"))

    def emit_q_half(l, half):
        for segc in range(2):
            sg = 2 * half + segc
            for c in range(8):
                wb = c % 2
                join((sT,), ("sync",))
                dma("sync", dW, WKV[:, wb], wsp_att[l, 0, c * 128:(c + 1) * 128, :])
                join((dW,), ("tensor",))
                def f(e, c=c, wb=wb, segc=segc):
                    ins = None
                    for h in range(H):
                        ins = e.matmul(PS[h][:], WKV[:, wb][:, h * 128:(h + 1) * 128],
                                       XNT[:, c * HALF + segc * 512:c * HALF + (segc + 1) * 512],
                                       start=(c == 0), stop=(c == 7))
                    return ins
                op("tensor", sT, f)
            for h in range(H):
                tb = h % 2
                dma("sync", dW, CKS[:, tb], dcq[:, sg * 512:(sg + 1) * 512])
                dma("sync", dW, SKS[:, tb], dsq[:, sg * 512:(sg + 1) * 512])
                op("vector", sV, lambda e, h=h, l=l, sg=sg: e.tensor_scalar(
                    RQS[:], RQCL[:, l * 8 + h:l * 8 + h + 1], float(qseg[h][sg]), None, ALU.mult))
                join((sT, sV), ("scalar",))
                op("scalar", sS, lambda e, h=h, sg=sg: e.activation(
                    ZROLL[:, 0:512], PS[h][:], AF.Identity, bias=RQS[:], scale=float(qseg[h][sg])))
                join((sS,), ("tensor",))
                op("tensor", sT, lambda e, h=h: e.matmul(
                    PS[h][:], PSW[:], ZROLL[:, 0:512], start=True, stop=True))
                join((sT, dW), ("vector",))
                def f(e, h=h, segc=segc, tb=tb):
                    e.tensor_tensor(ATS[:, 0], PS[h][:], SKS[:, tb], ALU.mult)
                    e.tensor_tensor(ATS[:, 1], ZROLL[:, 0:512], CKS[:, tb], ALU.mult)
                    return e.tensor_add(QT[:, h * HALF + segc * 512:h * HALF + (segc + 1) * 512],
                                        ATS[:, 0], ATS[:, 1])
                op("vector", sV, f)
                join((sV,), ("tensor", "scalar", "sync"))

    def emit_g_half(l, half):
        for qh in range(2):
            for c in range(8):
                wb = c % 2
                join((sT,), ("sync",))
                dma("sync", dW, WKV[:, wb], wsp_att[l, 3, c * 128:(c + 1) * 128, :])
                join((dW,), ("tensor",))
                def f(e, c=c, wb=wb, qh=qh):
                    ins = None
                    for h in range(H):
                        ins = e.matmul(PS[h][:], WKV[:, wb][:, h * 128:(h + 1) * 128],
                                       XNT[:, c * HALF + qh * 512:c * HALF + (qh + 1) * 512],
                                       start=(c == 0), stop=(c == 7))
                    return ins
                op("tensor", sT, f)
            join((sT,), ("scalar",))
            def f(e, qh=qh, l=l):
                ins = None
                for h in range(H):
                    ins = e.activation(GT[:, h * HALF + qh * 512:h * HALF + (qh + 1) * 512],
                                       PS[h][:], AF.Silu, bias=RGCL[:, l * 8 + h:l * 8 + h + 1])
                return ins
            op("scalar", sS, f)
            join((sS,), ("tensor",))

    def emit_v_half(l, half):
        for gj in range(2):
            for c in range(8):
                wb = c % 2
                join((sT,), ("sync",))
                dma("sync", dW, WKV[:, wb], wsp_att[l, 2, c * 128:(c + 1) * 128, :])
                join((dW,), ("tensor",))
                def f(e, c=c, wb=wb, gj=gj, half=half):
                    ins = None
                    for j4 in range(4):
                        mt = half * 8 + gj * 4 + j4
                        xs = XNT[:, c * HALF + (gj * 4 + j4) * 128:c * HALF + (gj * 4 + j4 + 1) * 128]
                        e.matmul(PS[j4 * 2][:], xs, WKV[:, wb][:, 0:512], start=(c == 0), stop=False)
                        ins = e.matmul(PS[j4 * 2 + 1][:], xs, WKV[:, wb][:, 512:1024],
                                       start=(c == 0), stop=False)
                    return ins
                op("tensor", sT, f)
            def f(e):
                ins = None
                for j4 in range(4):
                    e.matmul(PS[j4 * 2][:], ONR[:], RVR1[0:1, 0:512],
                             start=False, stop=True)
                    ins = e.matmul(PS[j4 * 2 + 1][:], ONR[:],
                                   RVR1[0:1, 512:1024], start=False, stop=True)
                return ins
            op("tensor", sT, f)
            join((sT,), ("scalar",))
            def f(e, gj=gj, half=half):
                ins = None
                for j4 in range(4):
                    mt = half * 8 + gj * 4 + j4
                    e.activation(VB[:, mt * 1024:mt * 1024 + 512], PS[j4 * 2][:], AF.Copy)
                    ins = e.activation(VB[:, mt * 1024 + 512:(mt + 1) * 1024],
                                       PS[j4 * 2 + 1][:], AF.Copy)
                return ins
            op("scalar", sS, f)
            join((sS,), ("tensor",))

    def emit_retention_half(l, half):
        for h in range(H):
            eb = h % 2
            dma("sync", dW, E2S[:, eb], de2[h])
            for segc in range(2):
                sg = 2 * half + segc
                n_mt = 4 * sg + 4
                ypb = PS[2 + segc]
                for mt in range(n_mt):
                    ab = mt % 2
                    if mt >= 2 or mt == 0:
                        join((sV, sS), ("tensor",))
                    op("tensor", sT, lambda e, h=h, mt=mt, ab=ab, segc=segc: e.matmul(
                        PS[ab][:], KT[:, h * S + mt * 128:h * S + (mt + 1) * 128],
                        QT[:, h * HALF + segc * 512:h * HALF + (segc + 1) * 512],
                        start=True, stop=True))
                    c = mt - 4 * sg
                    if c >= 0:
                        join((sT,), ("vector",))
                        op("vector", sV, lambda e, ab=ab, c=c: e.tensor_tensor(
                            ATS[:, ab], PS[ab][:], BM[:, 384 - 128 * c:896 - 128 * c], ALU.mult))
                        join((sV,), ("tensor",))
                    else:
                        join((sT,), ("scalar",))
                        op("scalar", sS, lambda e, ab=ab: e.activation(
                            ATS[:, ab], PS[ab][:], AF.Copy))
                        join((sS,), ("tensor",))
                    op("tensor", sT, lambda e, h=h, mt=mt, ab=ab, ypb=ypb, n_mt=n_mt: e.matmul(
                        ypb[:], VB[:, mt * 1024 + h * 128:mt * 1024 + (h + 1) * 128],
                        ATS[:, ab], start=(mt == 0), stop=(mt == n_mt - 1)))
                join((sT,), ("scalar",))
                def f(e, ypb=ypb):
                    e.activation(YSB[:], ypb[:], AF.Copy)
                    return e.activation(Y2B[:], ypb[:], AF.Square)
                op("scalar", sS, f)
                join((sS,), ("tensor",))
                def f(e):
                    e.matmul(PS[4][:], ON1[:], YSB[:], start=True, stop=True)
                    return e.matmul(PS[5][:], ON1[:], Y2B[:], start=True, stop=True)
                op("tensor", sT, f)
                join((sT,), ("scalar",))
                op("scalar", sS, lambda e: e.activation(MSQ[:], PS[4][:], AF.Square))
                join((sS, dW), ("vector",))
                def f(e, eb=eb):
                    e.tensor_sub(NMU[:], E2S[:, eb], MSQ[:])
                    e.tensor_add(NMU[:], PS[5][:], NMU[:])
                    return e.tensor_sub(UB[:], YSB[:], PS[4][:])
                op("vector", sV, f)
                join((sV,), ("scalar",))
                def f(e):
                    e.activation(NMU[:], NMU[:], AF.Ln)
                    return e.activation(NMU[:], NMU[:], AF.Exp, scale=-0.5)
                op("scalar", sS, f)
                join((sS,), ("vector",))
                def f(e, l=l, h=h, segc=segc):
                    gns = GNC[:, l * H + h:l * H + h + 1]
                    gnb = GNC[:, NL * H + l * H + h:NL * H + l * H + h + 1]
                    e.tensor_mul(UB[:], UB[:], NMU[:])
                    e.tensor_scalar(UB[:], UB[:], gns, gnb, ALU.mult, ALU.add)
                    zt = GT[:, h * HALF + segc * 512:h * HALF + (segc + 1) * 512]
                    return e.tensor_mul(zt, zt, UB[:])
                op("vector", sV, f)
                join((sV,), ("scalar", "tensor"))

    def emit_out_half(l, half):
        for gj in range(2):
            for c in range(8):
                wb = c % 2
                join((sT,), ("sync",))
                dma("sync", dW, WMV[:, wb], wsp_att[l, 4, c * 128:(c + 1) * 128, :])
                join((dW,), ("tensor",))
                def f(e, c=c, wb=wb, gj=gj):
                    ins = None
                    for j4 in range(4):
                        jl = gj * 4 + j4
                        zs = GT[:, c * HALF + jl * 128:c * HALF + (jl + 1) * 128]
                        e.matmul(PS[j4 * 2][:], zs, WMV[:, wb][:, 0:512], start=(c == 0), stop=(c == 7))
                        ins = e.matmul(PS[j4 * 2 + 1][:], zs, WMV[:, wb][:, 512:1024],
                                       start=(c == 0), stop=(c == 7))
                    return ins
                op("tensor", sT, f)
            join((sT,), ("vector",))
            def f(e, gj=gj, half=half):
                ins = None
                for j4 in range(4):
                    nt = half * 8 + gj * 4 + j4
                    rt = r_tile(nt)
                    e.tensor_add(rt[:, 0:512], PS[j4 * 2][:], rt[:, 0:512])
                    ins = e.tensor_add(rt[:, 512:1024], PS[j4 * 2 + 1][:], rt[:, 512:1024])
                return ins
            op("vector", sV, f)
            join((sV,), ("tensor",))

    def emit_ffn_inner_half(l, half):
        tiles = list(range(half * 8, half * 8 + 8))
        for nt in tiles:
            ln_stats_tile(r_tile(nt), nt)
        ln_finish(tiles)
        for j in range(8):
            nt = half * 8 + j
            if j > 0:
                join((dG,), ("scalar",))
            ln_affine(r_tile(nt), ZROLL[:], nt)
            join((sS,), ("scalar",))
            for c in range(8):
                dma("scalar", dG, X2NT[:, c * HALF + j * 128:c * HALF + (j + 1) * 128],
                    ZROLL[:, c * 128:(c + 1) * 128], transpose=True)
        join((dG,), ("tensor",))
        for fc in range(16):
            for c in range(8):
                wb = c % 2
                join((sT,), ("sync",))
                dma("sync", dW, WMV[:, wb][:, 0:128],
                    wsp_w1[l, c * 128:(c + 1) * 128, fc * 128:(fc + 1) * 128])
                join((dW,), ("tensor",))
                def f(e, c=c, wb=wb):
                    e.matmul(PS[0][:], WMV[:, wb][:, 0:128], XNT[:, c * HALF:c * HALF + 512],
                             start=(c == 0), stop=(c == 7))
                    return e.matmul(PS[1][:], WMV[:, wb][:, 0:128],
                                    XNT[:, c * HALF + 512:c * HALF + 1024],
                                    start=(c == 0), stop=(c == 7))
                op("tensor", sT, f)
            join((sT,), ("scalar",))
            def f(e, fc=fc, l=l):
                e.activation(H1T[:, fc * HALF:fc * HALF + 512], PS[0][:],
                             AF.Gelu_apprx_tanh, bias=RW1CL[:, l * 16 + fc:l * 16 + fc + 1])
                return e.activation(H1T[:, fc * HALF + 512:(fc + 1) * HALF], PS[1][:],
                                    AF.Gelu_apprx_tanh, bias=RW1CL[:, l * 16 + fc:l * 16 + fc + 1])
            op("scalar", sS, f)
            join((sS,), ("tensor",))
        for gj in range(2):
            for fc in range(16):
                wb = fc % 2
                join((sT,), ("sync",))
                dma("sync", dW, WMV[:, wb], wsp_w2[l, fc * 128:(fc + 1) * 128, :])
                join((dW,), ("tensor",))
                def f(e, fc=fc, wb=wb, gj=gj):
                    ins = None
                    for j4 in range(4):
                        jl = gj * 4 + j4
                        hs = H1T[:, fc * HALF + jl * 128:fc * HALF + (jl + 1) * 128]
                        e.matmul(PS[j4 * 2][:], hs, WMV[:, wb][:, 0:512],
                                 start=(fc == 0), stop=(fc == 15))
                        ins = e.matmul(PS[j4 * 2 + 1][:], hs, WMV[:, wb][:, 512:1024],
                                       start=(fc == 0), stop=(fc == 15))
                    return ins
                op("tensor", sT, f)
            join((sT,), ("vector",))
            def f(e, gj=gj, half=half):
                ins = None
                for j4 in range(4):
                    nt = half * 8 + gj * 4 + j4
                    rt = r_tile(nt)
                    e.tensor_add(rt[:, 0:512], PS[j4 * 2][:], rt[:, 0:512])
                    ins = e.tensor_add(rt[:, 512:1024], PS[j4 * 2 + 1][:], rt[:, 512:1024])
                return ins
            op("vector", sV, f)
            join((sV,), ("tensor",))

    def emit_layer(l, b):
        if b > 0:
            dma("sync", dW, RVR1[0:1, :], rv_dram[l:l + 1, :])
        for nt in range(NT):
            ln_stats_tile(r_tile(nt), nt)
        ln_finish(list(range(NT)))
        emit_xnt_half(0)
        join((dG,), ("tensor",))
        emit_k_half(l, 0)
        emit_v_half(l, 0)
        emit_q_half(l, 0)
        emit_g_half(l, 0)
        emit_xnt_half(1)
        join((dG,), ("tensor",))
        emit_k_half(l, 1)
        emit_v_half(l, 1)
        barrier()
        emit_retention_half(l, 0)
        emit_out_half(l, 0)
        emit_ffn_inner_half(l, 0)
        barrier()
        emit_xnt_half(1)
        join((dG,), ("tensor",))
        emit_q_half(l, 1)
        emit_g_half(l, 1)
        barrier()
        tap("qth1_l%d" % l, QT)
        tap("gth1_l%d" % l, GT)
        emit_retention_half(l, 1)
        barrier()
        tap("zth1_l%d" % l, GT)
        emit_out_half(l, 1)
        barrier()
        tap("yr_l%d" % l, R[:])
        emit_ffn_inner_half(l, 1)
        barrier()

    def emit_quant(src_ap, w_, dst_ap):
        nb_ = w_ // 16
        def f(e, src_ap=src_ap, w_=w_, nb_=nb_):
            gv = src_ap.rearrange("p (g k) -> p g k", k=16)
            e.tensor_reduce(QM[:, 0:nb_], gv, AX.X, ALU.max, apply_absolute_value=True)
            e.drain()
            e.tensor_scalar(QSH[:, 0:nb_], QM[:, 0:nb_], 1.0 / 12.0, 0.5e-12, ALU.mult, ALU.max)
            e.drain()
            e.tensor_scalar(QSC[:, 0:nb_], QSH[:, 0:nb_], 2.0, None, ALU.mult)
            e.drain()
            e.reciprocal(QU[:, 0:nb_], QSC[:, 0:nb_])
            e.drain()
            ub = QU[:, 0:nb_].rearrange("p (g u) -> p g u", u=1).to_broadcast([128, nb_, 16])
            return e.tensor_tensor(QY[:, 0:w_].rearrange("p (g k) -> p g k", k=16), gv, ub, ALU.mult)
        op("vector", sV, f)
        join((sV,), ("scalar",))
        def f(e, w_=w_):
            e.activation(QSG[:, 0:w_], QY[:, 0:w_], AF.Sign)
            return e.activation(QY[:, 0:w_], QY[:, 0:w_], AF.Abs)
        op("scalar", sS, f)
        join((sS,), ("vector",))
        def f(e, w_=w_, nb_=nb_, dst_ap=dst_ap):
            a = QY[:, 0:w_]
            e.tensor_single_scalar(QT1[:, 0:w_], a, 0.25, ALU.is_gt)
            e.scalar_tensor_tensor(QT1[:, 0:w_], a, 0.75, QT1[:, 0:w_], ALU.is_gt, ALU.add)
            e.scalar_tensor_tensor(QT1[:, 0:w_], a, 1.25, QT1[:, 0:w_], ALU.is_gt, ALU.add)
            e.scalar_tensor_tensor(QT1[:, 0:w_], a, 1.75, QT1[:, 0:w_], ALU.is_gt, ALU.add)
            e.tensor_single_scalar(QT2[:, 0:w_], a, 2.5, ALU.is_gt)
            e.scalar_tensor_tensor(QT2[:, 0:w_], a, 3.5, QT2[:, 0:w_], ALU.is_gt, ALU.add)
            e.scalar_tensor_tensor(QT1[:, 0:w_], QT2[:, 0:w_], 2.0, QT1[:, 0:w_], ALU.mult, ALU.add)
            e.tensor_scalar(QT2[:, 0:w_], a, 5.0, 4.0, ALU.is_gt, ALU.mult)
            e.tensor_add(QT1[:, 0:w_], QT1[:, 0:w_], QT2[:, 0:w_])
            e.tensor_mul(QT1[:, 0:w_], QT1[:, 0:w_], QSG[:, 0:w_])
            shb = QSH[:, 0:nb_].rearrange("p (g u) -> p g u", u=1).to_broadcast([128, nb_, 16])
            return e.tensor_tensor(dst_ap.rearrange("p (g k) -> p g k", k=16),
                                   QT1[:, 0:w_].rearrange("p (g k) -> p g k", k=16), shb, ALU.mult)
        op("vector", sV, f)

    def emit_quant_wide(src_ap_fn, w_total, dst_ap_fn):
        off = 0
        while off < w_total:
            cw = min(QCW, w_total - off)
            emit_quant(src_ap_fn(off, cw), cw, dst_ap_fn(off, cw))
            off += cw

    def emit_ffn_wprep():
        for wi, src in enumerate((dfg, dfu)):
            for r in range(22):
                buf = r % 2
                join((sV,), ("sync",))
                dma("sync", dW, FQR[:, buf], src[r * 128:(r + 1) * 128, :])
                join((dW,), ("vector",))
                emit_quant_wide(lambda o, cw, buf=buf: FQR[:, buf][:, o:o + cw], D,
                                lambda o, cw: QQ[:, o:o + cw])
                join((sV,), ("scalar",))
                for c in range(8):
                    dma("scalar", dG, TRT[:, c * 128:(c + 1) * 128],
                        QQ[:, c * 128:(c + 1) * 128], transpose=True)
                join((dG,), ("gpsimd",))
                for c in range(8):
                    dma("gpsimd", dG, wsp_fgu[wi, :, c * FO + r * 128:c * FO + (r + 1) * 128],
                        TRT[:, c * 128:(c + 1) * 128])
                join((dG,), ("vector",))
        for r in range(8):
            for cc, wdt in ((0, 1024), (1024, 1024), (2048, 768)):
                buf = (cc // 1024) % 2
                join((sV,), ("sync",))
                dma("sync", dW, FQR[:, buf][:, 0:wdt], dfd[r * 128:(r + 1) * 128, cc:cc + wdt])
                join((dW,), ("vector",))
                emit_quant_wide(lambda o, cw, buf=buf: FQR[:, buf][:, o:o + cw], wdt,
                                lambda o, cw, cc=cc: QQ[:, cc + o:cc + o + cw])
            join((sV,), ("scalar",))
            for c in range(22):
                dma("scalar", dG, TRT[:, (c % 8) * 128:(c % 8 + 1) * 128],
                    QQ[:, c * 128:(c + 1) * 128], transpose=True)
                join((dG,), ("gpsimd",))
                dma("gpsimd", dG, wsp_fd[:, c * D + r * 128:c * D + (r + 1) * 128],
                    TRT[:, (c % 8) * 128:(c % 8 + 1) * 128])
            join((dG,), ("vector",))
        barrier()

    def emit_ffn_outer(b):
        for nt in range(NT):
            buf = nt % 2
            join((sS, sV, dG), ("sync",))
            dma("sync", dW, XT32[:, buf], dx[b, nt * 128:(nt + 1) * 128, :])
            join((dW,), ("vector",))
            op("vector", sV, lambda e, nt=nt, buf=buf: e.tensor_add(
                XT32[:, buf], XT32[:, buf], r_tile(nt)))
            join((sV,), ("gpsimd",))
            dma("gpsimd", dG, x2_dram[nt * 128:(nt + 1) * 128, :], XT32[:, buf])
            ln_stats_tile(XT32[:, buf], nt)
            ln_finish([nt])
            join((dG,), ("scalar",))
            ln_affine(XT32[:, buf], XT32[:, buf], nt)
            join((sS,), ("vector",))
            emit_quant_wide(lambda o, cw, buf=buf: XT32[:, buf][:, o:o + cw], D,
                            lambda o, cw: QQ[:, o:o + cw])
            join((sV,), ("scalar",))
            join((dG,), ("scalar",))
            for c in range(8):
                dma("scalar", dG, XQT[:, c * S + nt * 128:c * S + (nt + 1) * 128],
                    QQ[:, c * 128:(c + 1) * 128], transpose=True)
            join((dG,), ("vector", "scalar"))
        tap("xqt_b%d" % b, XQT)
        fsegs = [(0, 512), (512, 512), (1024, 512), (1536, 512), (2048, 512), (2560, 256)]
        join((dG,), ("tensor",))
        for fi, (f0, fw) in enumerate(fsegs):
            wb_ = fi % 2
            join((sT,), ("sync",))
            for c in range(8):
                dma("sync", dW, FGW[wb_][:, c * 512:c * 512 + fw],
                    wsp_fgu[0, :, c * FO + f0:c * FO + f0 + fw])
                dma("sync", dW, FUW[wb_][:, c * 512:c * 512 + fw],
                    wsp_fgu[1, :, c * FO + f0:c * FO + f0 + fw])
            for nt in range(NT):
                join((dW,), ("tensor",))
                if nt > 0:
                    join((sS, sV), ("tensor",))
                def f(e, nt=nt, fw=fw, wb_=wb_):
                    ins = None
                    for c in range(8):
                        xs = XQT[:, c * S + nt * 128:c * S + (nt + 1) * 128]
                        e.matmul(PS[0][:, 0:fw], xs, FGW[wb_][:, c * 512:c * 512 + fw],
                                 start=(c == 0), stop=(c == 7))
                        ins = e.matmul(PS[1][:, 0:fw], xs, FUW[wb_][:, c * 512:c * 512 + fw],
                                       start=(c == 0), stop=(c == 7))
                    return ins
                op("tensor", sT, f)
                join((sT,), ("scalar",))
                op("scalar", sS, lambda e, fw=fw: e.activation(
                    FSIL[:, 0:fw], PS[0][:, 0:fw], AF.Silu))
                join((sS,), ("vector",))
                op("vector", sV, lambda e, fw=fw: e.tensor_tensor(
                    FH[:, 0:fw], PS[1][:, 0:fw], FSIL[:, 0:fw], ALU.mult))
                emit_quant_wide(lambda o, cw: FH[:, o:o + cw], fw,
                                lambda o, cw: QQ[:, o:o + cw])
                join((sV,), ("scalar",))
                if nt > 0:
                    join((dG,), ("scalar",))
                for c in range(fw // 128):
                    dma("scalar", dG, TRT[:, c * 128:(c + 1) * 128],
                        QQ[:, c * 128:(c + 1) * 128], transpose=True)
                join((dG,), ("gpsimd",))
                for c in range(fw // 128):
                    fc = (f0 + c * 128) // 128
                    dma("gpsimd", dG, hq_dram[:, fc * S + nt * 128:fc * S + (nt + 1) * 128],
                        TRT[:, c * 128:(c + 1) * 128])
                join((dG,), ("vector",))
        barrier()
        for dh2 in range(2):
            for c in range(22):
                dma("sync", dW, FDW[:, c * 512:(c + 1) * 512],
                    wsp_fd[:, c * D + dh2 * 512:c * D + dh2 * 512 + 512])
            barrier()
            if dh2 == 0:
                tap("fdw_b%d" % b, FDW)
            for nt in range(NT):
                hb = nt % 2
                join((sT, sV, dG), ("sync",))
                for c in range(22):
                    dma("sync", dW, FHQ[hb][:, c * 128:(c + 1) * 128],
                        hq_dram[:, c * S + nt * 128:c * S + (nt + 1) * 128])
                dma("sync", dW, XT32[:, hb][:, dh2 * 512:dh2 * 512 + 512],
                    x2_dram[nt * 128:(nt + 1) * 128, dh2 * 512:dh2 * 512 + 512])
                join((dW,), ("tensor",))
                def f(e, hb=hb):
                    ins = None
                    for c in range(22):
                        ins = e.matmul(PS[6][:], FHQ[hb][:, c * 128:(c + 1) * 128],
                                       FDW[:, c * 512:(c + 1) * 512],
                                       start=(c == 0), stop=(c == 21))
                    return ins
                op("tensor", sT, f)
                join((sT,), ("vector",))
                op("vector", sV, lambda e, hb=hb, dh2=dh2: e.tensor_add(
                    XT32[:, hb][:, dh2 * 512:dh2 * 512 + 512], PS[6][:],
                    XT32[:, hb][:, dh2 * 512:dh2 * 512 + 512]))
                join((sV,), ("tensor", "gpsimd"))
                dma("gpsimd", dG, dout[b, nt * 128:(nt + 1) * 128, dh2 * 512:dh2 * 512 + 512],
                    XT32[:, hb][:, dh2 * 512:dh2 * 512 + 512])
                join((dG,), ("sync",))
        barrier()

    for b in range(n_batches):
        emit_top_ln(b)
        barrier()
        tap("topln_b%d" % b, R[:])
        if do_attn:
            for l in range(n_layers):
                if b == 0:
                    emit_wprep_layer(l)
                emit_layer(l, b)
                tap("layer%d_b%d" % (l, b), R[:])
        tap("kt_b%d" % b, KT[:])
        tap("vb_b%d" % b, VB[:])
        tap("qt_b%d" % b, QT)
        tap("gt_b%d" % b, GT)
        if do_ffn:
            if b == 0:
                emit_ffn_wprep()
            emit_ffn_outer(b)
        elif True:
            for nt in range(NT):
                buf = nt % 2
                if nt >= 2:
                    join((dG,), ("vector",))
                op("vector", sV, lambda e, nt=nt, buf=buf: e.tensor_copy(
                    XT32[:, buf], r_tile(nt)))
                join((sV,), ("gpsimd",))
                dma("gpsimd", dG, dout[b, nt * 128:(nt + 1) * 128, :], XT32[:, buf])
                join((dG,), ("vector",))
            barrier()
    barrier()

    with nc.Block() as block:
        def runner(name):
            def f(eng):
                for fn in P.streams[name]:
                    fn(eng)
            return f
        block.sync(runner("sync"))
        block.tensor(runner("tensor"))
        block.vector(runner("vector"))
        block.scalar(runner("scalar"))
        block.gpsimd(runner("gpsimd"))

    ctx.close()
    return nc


_CFG = {}


def _prep_inputs(inputs):
    hc = host_consts()
    wq = np.asarray(inputs["L0_attn_Wq"])
    wk = np.asarray(inputs["L0_attn_Wk"])
    colperm = np.concatenate([h * 128 + PERM for h in range(H)])
    wq_p = np.ascontiguousarray(wq[:, :, colperm])
    wk_p = np.ascontiguousarray(wk[:, :, colperm])
    lng = np.ascontiguousarray(np.asarray(inputs["L0_attn_ln_scale"]).reshape(8, 128).T)
    lnb = np.ascontiguousarray(np.asarray(inputs["L0_attn_ln_bias"]).reshape(8, 128).T)
    gns = np.asarray(inputs["L0_attn_gn_scale"]).reshape(NL, H, 128)
    gnb = np.asarray(inputs["L0_attn_gn_bias"]).reshape(NL, H, 128)
    gn = np.concatenate([gns.reshape(NL * H, 128), gnb.reshape(NL * H, 128)], 0).T
    common = {
        "wq": wq_p.astype(np.float32),
        "wk": wk_p.astype(np.float32),
        "wv": np.ascontiguousarray(np.asarray(inputs["L0_attn_Wv"], dtype=np.float32)),
        "wg": np.ascontiguousarray(np.asarray(inputs["L0_attn_Wg"], dtype=np.float32)),
        "wo": np.ascontiguousarray(np.asarray(inputs["L0_attn_Wo"], dtype=np.float32)),
        "w1": np.ascontiguousarray(np.asarray(inputs["L0_attn_W1"], dtype=np.float32)),
        "w2": np.ascontiguousarray(np.asarray(inputs["L0_attn_W2"], dtype=np.float32)),
        "lng": np.ascontiguousarray(lng.astype(np.float32)),
        "lnb": np.ascontiguousarray(lnb.astype(np.float32)),
        "gn": np.ascontiguousarray(gn.astype(np.float32)),
        "wfg": np.ascontiguousarray(np.asarray(inputs["L0_ffn_w_gate"], dtype=np.float32)),
        "wfu": np.ascontiguousarray(np.asarray(inputs["L0_ffn_w_up"], dtype=np.float32)),
        "wfd": np.ascontiguousarray(np.asarray(inputs["L0_ffn_w_down"], dtype=np.float32)),
        "cq": hc["cq"], "sq": hc["sq"], "ckt": hc["ckt"], "skt": hc["skt"],
        "e2t": hc["e2t"], "bigmask": hc["bigmask"], "ones128": hc["ones128"],
        "onesrow": hc["onesrow"], "pswap": hc["pswap"],
    }
    x = np.asarray(inputs["L0_attn_X"], dtype=np.float32)
    in_maps = []
    for c in range(8):
        m = dict(common)
        m["x"] = np.ascontiguousarray(x[2 * c:2 * c + 2])
        in_maps.append(m)
    return in_maps, hc


def kernel(**inputs):
    in_maps, hc = _prep_inputs(inputs)
    cfg = dict(_CFG)
    cfg["qseg"] = hc["qseg"]
    nc = build_program(cfg)
    res = run_bass_kernel_spmd(nc, in_maps, core_ids=list(range(8)))
    out = np.concatenate([np.asarray(r["out"]) for r in res.results], 0)
    return out.reshape(B, S, D).astype(np.float32)


# version 02
# speedup vs baseline: 1.0178x; 1.0178x over previous
import numpy as np
import ml_dtypes
import concourse.bass as bass
import concourse.mybir as mybir
from concourse.bass_utils import run_bass_kernel_spmd

F32 = mybir.dt.float32
BF16 = mybir.dt.bfloat16
AF = mybir.ActivationFunctionType
ALU = mybir.AluOpType
AX = mybir.AxisListType

B, S, D, H, DH, NL = 16, 2048, 1024, 8, 128, 4
FI = 2048
FO = 2816
EPS = 1e-5
NB = 2
NT = 16
HALF = 1024
QCW = 512

PERM = np.concatenate([np.arange(0, 128, 2), np.arange(1, 128, 2)])


def host_consts():
    c = {}
    gam = 1.0 - np.exp(np.linspace(np.log(1 / 32), np.log(1 / 512), H)).astype(np.float64)
    invf = 1.0 / 10000.0 ** (np.arange(0, DH, 2, dtype=np.float64) / DH)
    n = np.arange(S, dtype=np.float64)
    ang = n[:, None] * invf[None, :]
    cos, sin = np.cos(ang), np.sin(ang)
    CQ = np.concatenate([cos.T, cos.T], 0)
    SQ = np.concatenate([-sin.T, sin.T], 0)
    c["cq"] = CQ.astype(ml_dtypes.bfloat16)
    c["sq"] = SQ.astype(ml_dtypes.bfloat16)
    ckt = np.zeros((H, 128, S), dtype=np.float64)
    skt = np.zeros((H, 128, S), dtype=np.float64)
    SKbase = np.concatenate([sin.T, -sin.T], 0)
    for h in range(H):
        gn = gam[h] ** (-n)
        ckt[h] = CQ * gn[None, :]
        skt[h] = SKbase * gn[None, :]
    c["ckt"] = ckt.astype(ml_dtypes.bfloat16)
    c["skt"] = skt.astype(ml_dtypes.bfloat16)
    dl = np.arange(512, dtype=np.float64)
    e2 = np.stack([EPS * gam[h] ** (-2.0 * dl) for h in range(H)], 0)
    c["e2t"] = np.broadcast_to(e2[:, None, :], (H, 128, 512)).astype(np.float32).copy()
    c["qseg"] = [[float(gam[h] ** (512.0 * t)) for t in range(4)] for h in range(H)]
    i_ = np.arange(128)[:, None]
    t_ = np.arange(896)[None, :]
    c["bigmask"] = (t_ >= i_ + 384).astype(ml_dtypes.bfloat16)
    c["ones128"] = np.full((128, 128), 1.0 / 128.0, dtype=ml_dtypes.bfloat16)
    c["onesrow"] = np.ones((1, 128), dtype=ml_dtypes.bfloat16)
    pm = np.zeros((128, 128), dtype=np.float32)
    pm[np.arange(128), (np.arange(128) + 64) % 128] = 1.0
    c["pswap"] = pm.astype(ml_dtypes.bfloat16)
    return c


class Sem:
    def __init__(self, h):
        self.h, self.n = h, 0


class Prog:
    def __init__(self):
        self.streams = {e: [] for e in ("sync", "tensor", "vector", "scalar", "gpsimd")}

    def emit(self, eng, fn):
        self.streams[eng].append(fn)


def build_program(cfg):
    n_layers = cfg.get("n_layers", NL)
    do_attn = cfg.get("do_attn", True)
    do_ffn = cfg.get("do_ffn", True)
    n_batches = cfg.get("n_batches", NB)
    tap_points = cfg.get("taps", {})
    qseg = cfg["qseg"]

    nc = bass.Bass("TRN2", target_bir_lowering=False)

    dx = nc.dram_tensor("x", [NB, S, D], F32, kind="ExternalInput").ap()
    dwa = [nc.dram_tensor(nm, [NL, D, D], F32, kind="ExternalInput").ap()
           for nm in ("wq", "wk", "wv", "wg", "wo")]
    dw1 = nc.dram_tensor("w1", [NL, D, FI], F32, kind="ExternalInput").ap()
    dw2 = nc.dram_tensor("w2", [NL, FI, D], F32, kind="ExternalInput").ap()
    dlng = nc.dram_tensor("lng", [128, 8], F32, kind="ExternalInput").ap()
    dlnb = nc.dram_tensor("lnb", [128, 8], F32, kind="ExternalInput").ap()
    dgn = nc.dram_tensor("gn", [128, 2 * NL * H], F32, kind="ExternalInput").ap()
    dfg = nc.dram_tensor("wfg", [FO, D], F32, kind="ExternalInput").ap()
    dfu = nc.dram_tensor("wfu", [FO, D], F32, kind="ExternalInput").ap()
    dfd = nc.dram_tensor("wfd", [D, FO], F32, kind="ExternalInput").ap()
    dcq = nc.dram_tensor("cq", [128, S], BF16, kind="ExternalInput").ap()
    dsq = nc.dram_tensor("sq", [128, S], BF16, kind="ExternalInput").ap()
    dck = nc.dram_tensor("ckt", [H, 128, S], BF16, kind="ExternalInput").ap()
    dsk = nc.dram_tensor("skt", [H, 128, S], BF16, kind="ExternalInput").ap()
    de2 = nc.dram_tensor("e2t", [H, 128, 512], F32, kind="ExternalInput").ap()
    dbm = nc.dram_tensor("bigmask", [128, 896], BF16, kind="ExternalInput").ap()
    don = nc.dram_tensor("ones128", [128, 128], BF16, kind="ExternalInput").ap()
    dor = nc.dram_tensor("onesrow", [1, 128], BF16, kind="ExternalInput").ap()
    dpm = nc.dram_tensor("pswap", [128, 128], BF16, kind="ExternalInput").ap()
    dout = nc.dram_tensor("out", [NB, S, D], F32, kind="ExternalOutput").ap()

    wsp_att = nc.dram_tensor("wsp_att", [NL, 5, D, D], BF16, kind="Internal").ap()
    wsp_w1 = nc.dram_tensor("wsp_w1", [NL, D, FI], BF16, kind="Internal").ap()
    wsp_w2 = nc.dram_tensor("wsp_w2", [NL, FI, D], BF16, kind="Internal").ap()
    wsp_fgu = nc.dram_tensor("wsp_fgu", [2, 128, 8 * FO], BF16, kind="Internal").ap()
    wsp_fd = nc.dram_tensor("wsp_fd", [128, 22 * D], BF16, kind="Internal").ap()
    x2_dram = nc.dram_tensor("x2s", [S, D], F32, kind="Internal").ap()
    rv_dram = nc.dram_tensor("rvs", [NL, 1024], BF16, kind="Internal").ap()
    hq_dram = nc.dram_tensor("hqs", [128, 22 * S], BF16, kind="Internal").ap()

    dtaps = {}
    for pname, (shape, dt) in tap_points.items():
        dtaps[pname] = nc.dram_tensor("tap_" + pname, shape, dt, kind="ExternalOutput").ap()

    from contextlib import ExitStack
    ctx = ExitStack()
    E = ctx.enter_context

    R = E(nc.sbuf_tensor("R", [128, 16384], BF16))
    KT = E(nc.sbuf_tensor("KT", [128, 16384], BF16))
    VB = E(nc.sbuf_tensor("VB", [128, 16384], BF16))
    QGH = E(nc.sbuf_tensor("QGH", [128, 16384], BF16))
    XNT = E(nc.sbuf_tensor("XNT", [128, 8192], BF16))
    QT = QGH[:, 0:8192]
    GT = QGH[:, 8192:16384]
    H1T = QGH[:, 0:16384]
    X2NT = XNT[:, 0:8192]
    XQT = KT[:, 0:16384]
    FDW = VB[:, 0:11264]
    QQ = VB[:, 11264 + 0:11264 + 2816]
    FGW = [QGH[:, 0:4096], QGH[:, 4096:8192]]
    FUW = [QGH[:, 8192:12288], QGH[:, 12288:16384]]
    FHQ = [XNT[:, 0:2816], XNT[:, 2816:5632]]

    CKS = E(nc.sbuf_tensor("CKS", [128, 2, 512], BF16))
    SKS = E(nc.sbuf_tensor("SKS", [128, 2, 512], BF16))
    E2S = E(nc.sbuf_tensor("E2S", [128, 2, 512], F32))
    BM = E(nc.sbuf_tensor("BM", [128, 896], BF16))
    ON1 = E(nc.sbuf_tensor("ON1", [128, 128], BF16))
    ONR = E(nc.sbuf_tensor("ONR", [1, 128], BF16))
    PSW = E(nc.sbuf_tensor("PSW", [128, 128], BF16))
    GCOL = E(nc.sbuf_tensor("GCOL", [128, 8], F32))
    BCOL = E(nc.sbuf_tensor("BCOL", [128, 8], F32))
    BCOLB = E(nc.sbuf_tensor("BCOLB", [128, 8], BF16))
    GNC = E(nc.sbuf_tensor("GNC", [128, 2 * NL * H], F32))
    RQCL = E(nc.sbuf_tensor("RQCL", [128, NL * 8], F32))
    RKCL = E(nc.sbuf_tensor("RKCL", [128, NL * 8], F32))
    RGCL = E(nc.sbuf_tensor("RGCL", [128, NL * 8], F32))
    RW1CL = E(nc.sbuf_tensor("RW1CL", [128, NL * 16], F32))
    RVR1 = E(nc.sbuf_tensor("RVR1", [1, 1024], BF16))
    RQS = E(nc.sbuf_tensor("RQS", [128, 1], F32))
    WF32 = E(nc.sbuf_tensor("WF32", [128, 2, 1024], F32))
    WPL = E(nc.sbuf_tensor("WPL", [128, 1024], BF16))
    WSC = E(nc.sbuf_tensor("WSC", [128, 2, 1024], BF16))
    WKV = E(nc.sbuf_tensor("WKV", [128, 2, 1024], BF16))
    WMV = E(nc.sbuf_tensor("WMV", [128, 2, 1024], BF16))
    XT32 = E(nc.sbuf_tensor("XT32", [128, 2, 1024], F32))
    ZROLL = E(nc.sbuf_tensor("ZROLL", [128, 1024], BF16))
    STATS = E(nc.sbuf_tensor("STATS", [128, NT * 12], F32))
    MU = E(nc.sbuf_tensor("MU", [128, NT], F32))
    VAR = E(nc.sbuf_tensor("VAR", [128, NT], F32))
    RSTD = E(nc.sbuf_tensor("RSTD", [128, NT], F32))
    NMR = E(nc.sbuf_tensor("NMR", [128, NT], F32))
    AGG2 = E(nc.sbuf_tensor("AGG2", [128, 2], F32))
    ATS = E(nc.sbuf_tensor("ATS", [128, 2, 512], BF16))
    YSB = E(nc.sbuf_tensor("YSB", [128, 512], BF16))
    Y2B = E(nc.sbuf_tensor("Y2B", [128, 512], BF16))
    MSQ = E(nc.sbuf_tensor("MSQ", [128, 512], BF16))
    UB = E(nc.sbuf_tensor("UB", [128, 512], BF16))
    NMU = E(nc.sbuf_tensor("NMU", [128, 512], F32))
    CAPC = E(nc.sbuf_tensor("CAPC", [128, 2], F32))
    QM = E(nc.sbuf_tensor("QM", [128, 32], F32))
    QSH = E(nc.sbuf_tensor("QSH", [128, 32], F32))
    QSC = E(nc.sbuf_tensor("QSC", [128, 32], F32))
    QU = E(nc.sbuf_tensor("QU", [128, 32], F32))
    QY = E(nc.sbuf_tensor("QY", [128, QCW], F32))
    QT1 = E(nc.sbuf_tensor("QT1", [128, QCW], F32))
    QT2 = E(nc.sbuf_tensor("QT2", [128, QCW], F32))
    QSG = E(nc.sbuf_tensor("QSG", [128, QCW], F32))
    FQR = XT32
    FH = NMU
    FSIL = QT2
    TRT = ZROLL

    PS = [E(nc.psum_tensor("ps%d" % i, [128, 512], F32)) for i in range(8)]

    sems = {}
    for nm in ("dW", "dG", "sV", "sS", "sT"):
        sems[nm] = Sem(E(nc.semaphore(nm)))
    dW, dG, sV, sS, sT = (sems[k] for k in ("dW", "dG", "sV", "sS", "sT"))

    P = Prog()

    def w(eng, sem, val=None):
        v = sem.n if val is None else val
        P.emit(eng, lambda e, sem=sem, v=v: e.wait_ge(sem.h, v))

    def join(producer_sems, consumer_engs):
        for eng in consumer_engs:
            for sem in producer_sems:
                w(eng, sem)

    def op(eng, sem, fn, k=1):
        sem.n += k
        P.emit(eng, lambda e, fn=fn, sem=sem, k=k: fn(e).then_inc(sem.h, k))
        return sem.n

    def opn(eng, fn):
        P.emit(eng, fn)

    def dma(eng_name, sem, dst, src, transpose=False):
        if transpose:
            return op(eng_name, sem, lambda e, d=dst, s=src: e.dma_start_transpose(d, s), k=16)
        return op(eng_name, sem, lambda e, d=dst, s=src: e.dma_start(d, s), k=16)

    def wv(eng, sem, val):
        assert val <= sem.n
        if val > 0:
            P.emit(eng, lambda e, sem=sem, v=val: e.wait_ge(sem.h, v))

    def barrier():
        join((dW, dG, sV, sS, sT), ("sync", "tensor", "vector", "scalar", "gpsimd"))

    def tap(pname, buf_ap):
        if pname in dtaps:
            barrier()
            dma("gpsimd", dG, dtaps[pname], buf_ap)
            barrier()

    nc.const_aps.register(F32, float(EPS), CAPC[:, 0:1])
    nc.const_aps.register(F32, -0.5, CAPC[:, 1:2])

    for dst, src in ((BM[:], dbm), (ON1[:], don), (ONR[:], dor), (PSW[:], dpm),
                     (GCOL[:], dlng), (BCOL[:], dlnb), (GNC[:], dgn)):
        dma("sync", dW, dst, src)
    join((dW,), ("vector",))
    op("vector", sV, lambda e: e.tensor_copy(BCOLB[:], BCOL[:]))
    def f(e):
        e.memset(CAPC[:, 0:1], float(EPS))
        return e.memset(CAPC[:, 1:2], -0.5)
    op("vector", sV, f)
    join((sV,), ("scalar", "tensor", "gpsimd", "sync"))

    def emit_wprep_layer(l):
        for wi in range(5):
            folded = wi < 4
            for c in range(8):
                buf = c % 2
                join((sV,) + ((sS,) if folded else ()), ("sync",))
                dma("sync", dW, WF32[:, buf], dwa[wi][l, c * 128:(c + 1) * 128, :])
                if folded:
                    join((dW,), ("scalar",))
                    if c > 0:
                        join((sT,), ("scalar",))
                    op("scalar", sS, lambda e, buf=buf: e.copy(WPL[:], WF32[:, buf]))
                    join((sS,), ("tensor",))
                    if wi == 2:
                        def f(e, c=c):
                            e.matmul(PS[0][0:1, :], BCOLB[:, c:c + 1], WPL[:, 0:512],
                                     start=(c == 0), stop=(c == 7))
                            return e.matmul(PS[1][0:1, :], BCOLB[:, c:c + 1], WPL[:, 512:1024],
                                            start=(c == 0), stop=(c == 7))
                        op("tensor", sT, f)
                    else:
                        def f(e, c=c):
                            ins = None
                            for h in range(H):
                                ins = e.matmul(PS[h][:, 0:1], WPL[:, h * 128:(h + 1) * 128],
                                               BCOLB[:, c:c + 1], start=(c == 0), stop=(c == 7))
                            return ins
                        op("tensor", sT, f)
                    join((dW,), ("vector",))
                    op("vector", sV, lambda e, buf=buf, c=c: e.tensor_scalar(
                        WSC[:, buf], WF32[:, buf], GCOL[:, c:c + 1], None, ALU.mult))
                else:
                    join((dW,), ("vector",))
                    op("vector", sV, lambda e, buf=buf: e.tensor_copy(WSC[:, buf], WF32[:, buf]))
                join((sV,), ("gpsimd",))
                if c >= 2:
                    join((dG,), ("vector",))
                dma("gpsimd", dG, wsp_att[l, wi, c * 128:(c + 1) * 128, :], WSC[:, buf])
            if folded:
                join((sT,), ("vector",))
                if wi == 2:
                    def f(e):
                        e.tensor_copy(RVR1[0:1, 0:512], PS[0][0:1, :])
                        return e.tensor_copy(RVR1[0:1, 512:1024], PS[1][0:1, :])
                    op("vector", sV, f)
                    join((sV,), ("gpsimd",))
                    dma("gpsimd", dG, rv_dram[l:l + 1, :], RVR1[0:1, :])
                else:
                    dstc = {0: RQCL, 1: RKCL, 3: RGCL}[wi]
                    def f(e, dstc=dstc, l=l):
                        ins = None
                        for h in range(H):
                            ins = e.tensor_copy(dstc[:, l * 8 + h:l * 8 + h + 1], PS[h][:, 0:1])
                        return ins
                    op("vector", sV, f)
                join((sV,), ("tensor",))
        for ch in range(2):
            for c in range(8):
                buf = c % 2
                join((sV, sS), ("sync",))
                dma("sync", dW, WF32[:, buf], dw1[l, c * 128:(c + 1) * 128, ch * 1024:(ch + 1) * 1024])
                join((dW,), ("scalar",))
                if c > 0 or ch > 0:
                    join((sT,), ("scalar",))
                op("scalar", sS, lambda e, buf=buf: e.copy(WPL[:], WF32[:, buf]))
                join((sS,), ("tensor",))
                def f(e, c=c):
                    ins = None
                    for h in range(H):
                        ins = e.matmul(PS[h][:, 0:1], WPL[:, h * 128:(h + 1) * 128],
                                       BCOLB[:, c:c + 1], start=(c == 0), stop=(c == 7))
                    return ins
                op("tensor", sT, f)
                join((dW,), ("vector",))
                op("vector", sV, lambda e, buf=buf, c=c: e.tensor_scalar(
                    WSC[:, buf], WF32[:, buf], GCOL[:, c:c + 1], None, ALU.mult))
                join((sV,), ("gpsimd",))
                if c >= 2 or ch > 0:
                    join((dG,), ("vector",))
                dma("gpsimd", dG, wsp_w1[l, c * 128:(c + 1) * 128, ch * 1024:(ch + 1) * 1024], WSC[:, buf])
            join((sT,), ("vector",))
            def f(e, l=l, ch=ch):
                ins = None
                for h in range(H):
                    ins = e.tensor_copy(RW1CL[:, l * 16 + ch * 8 + h:l * 16 + ch * 8 + h + 1],
                                        PS[h][:, 0:1])
                return ins
            op("vector", sV, f)
            join((sV,), ("tensor",))
        for c in range(16):
            buf = c % 2
            join((sV,), ("sync",))
            join((dG,), ("vector",))
            dma("sync", dW, WF32[:, buf], dw2[l, c * 128:(c + 1) * 128, :])
            join((dW,), ("vector",))
            op("vector", sV, lambda e, buf=buf: e.tensor_copy(WSC[:, buf], WF32[:, buf]))
            join((sV,), ("gpsimd",))
            dma("gpsimd", dG, wsp_w2[l, c * 128:(c + 1) * 128, :], WSC[:, buf])
        barrier()

    def ln_stats_tile(src_ap, nt):
        def f(e, src_ap=src_ap, nt=nt):
            st = STATS[:, nt * 12:(nt + 1) * 12]
            e.bn_stats(st[:, 0:6], src_ap[:, 0:512])
            e.bn_stats(st[:, 6:12], src_ap[:, 512:1024])
            e.drain()
            e.bn_aggr(AGG2[:], st)
            e.drain()
            e.tensor_copy(MU[:, nt:nt + 1], AGG2[:, 0:1])
            return e.tensor_copy(VAR[:, nt:nt + 1], AGG2[:, 1:2])
        op("vector", sV, f)

    def ln_finish(tiles):
        lo, hi = min(tiles), max(tiles) + 1
        join((sV,), ("scalar",))
        def f(e, lo=lo, hi=hi):
            e.activation(RSTD[:, lo:hi], VAR[:, lo:hi], AF.Ln, bias=float(EPS))
            return e.activation(RSTD[:, lo:hi], RSTD[:, lo:hi], AF.Exp, scale=-0.5)
        op("scalar", sS, f)
        join((sS,), ("vector",))
        op("vector", sV, lambda e, lo=lo, hi=hi: e.scalar_tensor_tensor(
            NMR[:, lo:hi], MU[:, lo:hi], -1.0, RSTD[:, lo:hi], ALU.mult, ALU.mult))
        join((sV,), ("scalar",))

    def ln_affine(src_ap, dst_ap, nt):
        op("scalar", sS, lambda e, src_ap=src_ap, dst_ap=dst_ap, nt=nt: e.activation(
            dst_ap, src_ap, AF.Identity, bias=NMR[:, nt:nt + 1], scale=RSTD[:, nt:nt + 1]))

    def r_tile(nt):
        return R[:, nt * 1024:(nt + 1) * 1024]

    def emit_top_ln(b):
        for nt in range(NT):
            buf = nt % 2
            join((sV, sS), ("sync",))
            dma("sync", dW, XT32[:, buf], dx[b, nt * 128:(nt + 1) * 128, :])
            join((dW,), ("vector",))
            ln_stats_tile(XT32[:, buf], nt)
            ln_finish([nt])
            ln_affine(XT32[:, buf], r_tile(nt), nt)

    def emit_xnt_half(half):
        for j in range(8):
            nt = half * 8 + j
            if j > 0:
                join((dG,), ("scalar",))
            ln_affine(r_tile(nt), ZROLL[:], nt)
            join((sS,), ("scalar",))
            for c in range(8):
                dma("scalar", dG, XNT[:, c * HALF + j * 128:c * HALF + (j + 1) * 128],
                    ZROLL[:, c * 128:(c + 1) * 128], transpose=True)

    def emit_k_half(l, half):
        for segc in range(2):
            sg = 2 * half + segc
            for c in range(8):
                wb = c % 2
                join((sT,), ("sync",))
                dma("sync", dW, WKV[:, wb], wsp_att[l, 1, c * 128:(c + 1) * 128, :])
                join((dW,), ("tensor",))
                def f(e, c=c, wb=wb, segc=segc):
                    ins = None
                    for h in range(H):
                        ins = e.matmul(PS[h][:], WKV[:, wb][:, h * 128:(h + 1) * 128],
                                       XNT[:, c * HALF + segc * 512:c * HALF + (segc + 1) * 512],
                                       start=(c == 0), stop=(c == 7))
                    return ins
                op("tensor", sT, f)
            for h in range(H):
                tb = h % 2
                dma("sync", dW, CKS[:, tb], dck[h, :, sg * 512:(sg + 1) * 512])
                dma("sync", dW, SKS[:, tb], dsk[h, :, sg * 512:(sg + 1) * 512])
                join((sT,), ("scalar",))
                if h > 0:
                    join((sT,), ("scalar",))
                op("scalar", sS, lambda e, h=h, l=l: e.activation(
                    ZROLL[:, 0:512], PS[h][:], AF.Identity, bias=RKCL[:, l * 8 + h:l * 8 + h + 1]))
                join((sS,), ("tensor",))
                op("tensor", sT, lambda e, h=h: e.matmul(
                    PS[h][:], PSW[:], ZROLL[:, 0:512], start=True, stop=True))
                join((sT, dW), ("vector",))
                def f(e, h=h, sg=sg, tb=tb):
                    e.tensor_tensor(ATS[:, 0], PS[h][:], SKS[:, tb], ALU.mult)
                    e.tensor_tensor(ATS[:, 1], ZROLL[:, 0:512], CKS[:, tb], ALU.mult)
                    return e.tensor_add(KT[:, h * S + sg * 512:h * S + (sg + 1) * 512],
                                        ATS[:, 0], ATS[:, 1])
                op("vector", sV, f)
                join((sV,), ("tensor", "scalar", "sync"))

    def emit_q_half(l, half):
        for segc in range(2):
            sg = 2 * half + segc
            for c in range(8):
                wb = c % 2
                join((sT,), ("sync",))
                dma("sync", dW, WKV[:, wb], wsp_att[l, 0, c * 128:(c + 1) * 128, :])
                join((dW,), ("tensor",))
                def f(e, c=c, wb=wb, segc=segc):
                    ins = None
                    for h in range(H):
                        ins = e.matmul(PS[h][:], WKV[:, wb][:, h * 128:(h + 1) * 128],
                                       XNT[:, c * HALF + segc * 512:c * HALF + (segc + 1) * 512],
                                       start=(c == 0), stop=(c == 7))
                    return ins
                op("tensor", sT, f)
            for h in range(H):
                tb = h % 2
                dma("sync", dW, CKS[:, tb], dcq[:, sg * 512:(sg + 1) * 512])
                dma("sync", dW, SKS[:, tb], dsq[:, sg * 512:(sg + 1) * 512])
                op("vector", sV, lambda e, h=h, l=l, sg=sg: e.tensor_scalar(
                    RQS[:], RQCL[:, l * 8 + h:l * 8 + h + 1], float(qseg[h][sg]), None, ALU.mult))
                join((sT, sV), ("scalar",))
                op("scalar", sS, lambda e, h=h, sg=sg: e.activation(
                    ZROLL[:, 0:512], PS[h][:], AF.Identity, bias=RQS[:], scale=float(qseg[h][sg])))
                join((sS,), ("tensor",))
                op("tensor", sT, lambda e, h=h: e.matmul(
                    PS[h][:], PSW[:], ZROLL[:, 0:512], start=True, stop=True))
                join((sT, dW), ("vector",))
                def f(e, h=h, segc=segc, tb=tb):
                    e.tensor_tensor(ATS[:, 0], PS[h][:], SKS[:, tb], ALU.mult)
                    e.tensor_tensor(ATS[:, 1], ZROLL[:, 0:512], CKS[:, tb], ALU.mult)
                    return e.tensor_add(QT[:, h * HALF + segc * 512:h * HALF + (segc + 1) * 512],
                                        ATS[:, 0], ATS[:, 1])
                op("vector", sV, f)
                join((sV,), ("tensor", "scalar", "sync"))

    def emit_g_half(l, half):
        for qh in range(2):
            for c in range(8):
                wb = c % 2
                join((sT,), ("sync",))
                dma("sync", dW, WKV[:, wb], wsp_att[l, 3, c * 128:(c + 1) * 128, :])
                join((dW,), ("tensor",))
                def f(e, c=c, wb=wb, qh=qh):
                    ins = None
                    for h in range(H):
                        ins = e.matmul(PS[h][:], WKV[:, wb][:, h * 128:(h + 1) * 128],
                                       XNT[:, c * HALF + qh * 512:c * HALF + (qh + 1) * 512],
                                       start=(c == 0), stop=(c == 7))
                    return ins
                op("tensor", sT, f)
            join((sT,), ("scalar",))
            def f(e, qh=qh, l=l):
                ins = None
                for h in range(H):
                    ins = e.activation(GT[:, h * HALF + qh * 512:h * HALF + (qh + 1) * 512],
                                       PS[h][:], AF.Silu, bias=RGCL[:, l * 8 + h:l * 8 + h + 1])
                return ins
            op("scalar", sS, f)
            join((sS,), ("tensor",))

    def emit_v_half(l, half):
        for gj in range(2):
            for c in range(8):
                wb = c % 2
                join((sT,), ("sync",))
                dma("sync", dW, WKV[:, wb], wsp_att[l, 2, c * 128:(c + 1) * 128, :])
                join((dW,), ("tensor",))
                def f(e, c=c, wb=wb, gj=gj, half=half):
                    ins = None
                    for j4 in range(4):
                        mt = half * 8 + gj * 4 + j4
                        xs = XNT[:, c * HALF + (gj * 4 + j4) * 128:c * HALF + (gj * 4 + j4 + 1) * 128]
                        e.matmul(PS[j4 * 2][:], xs, WKV[:, wb][:, 0:512], start=(c == 0), stop=False)
                        ins = e.matmul(PS[j4 * 2 + 1][:], xs, WKV[:, wb][:, 512:1024],
                                       start=(c == 0), stop=False)
                    return ins
                op("tensor", sT, f)
            def f(e):
                ins = None
                for j4 in range(4):
                    e.matmul(PS[j4 * 2][:], ONR[:], RVR1[0:1, 0:512],
                             start=False, stop=True)
                    ins = e.matmul(PS[j4 * 2 + 1][:], ONR[:],
                                   RVR1[0:1, 512:1024], start=False, stop=True)
                return ins
            op("tensor", sT, f)
            join((sT,), ("scalar",))
            def f(e, gj=gj, half=half):
                ins = None
                for j4 in range(4):
                    mt = half * 8 + gj * 4 + j4
                    e.activation(VB[:, mt * 1024:mt * 1024 + 512], PS[j4 * 2][:], AF.Copy)
                    ins = e.activation(VB[:, mt * 1024 + 512:(mt + 1) * 1024],
                                       PS[j4 * 2 + 1][:], AF.Copy)
                return ins
            op("scalar", sS, f)
            join((sS,), ("tensor",))

    def emit_retention_half(l, half):
        barrier()
        units = []
        for h in range(H):
            for segc in range(2):
                units.append((h, segc))
        gn_pend = None
        evd = {0: 0, 1: 0}
        evs = {0: sV, 1: sV}
        ydone = {0: 0, 1: 0}
        stats_read = 0
        last_t1 = 0
        e2d = {}

        def gn_tail(u, ui):
            nonlocal stats_read, last_t1
            h, segc, ypb, ylast = u
            eb = h % 2
            wv("scalar", sT, max(ylast, last_t1))
            s1 = op("scalar", sS, lambda e, ypb=ypb: (
                e.activation(YSB[:], ypb[:], AF.Copy),
                e.activation(Y2B[:], ypb[:], AF.Square))[-1])
            ydone[ui % 2] = s1
            wv("tensor", sS, s1)
            wv("tensor", sV, stats_read)
            t1 = op("tensor", sT, lambda e: (
                e.matmul(PS[4][:], ON1[:], YSB[:], start=True, stop=True),
                e.matmul(PS[5][:], ON1[:], Y2B[:], start=True, stop=True))[-1])
            last_t1 = t1
            wv("scalar", sT, t1)
            s2 = op("scalar", sS, lambda e: e.activation(MSQ[:], PS[4][:], AF.Square))
            wv("vector", sS, s2)
            wv("vector", dW, e2d[h])
            v1 = op("vector", sV, lambda e, eb=eb: (
                e.tensor_sub(NMU[:], E2S[:, eb], MSQ[:]),
                e.tensor_add(NMU[:], PS[5][:], NMU[:]),
                e.tensor_sub(UB[:], YSB[:], PS[4][:]))[-1])
            stats_read = v1
            wv("scalar", sV, v1)
            s3 = op("scalar", sS, lambda e: (
                e.activation(NMU[:], NMU[:], AF.Ln),
                e.activation(NMU[:], NMU[:], AF.Exp, scale=-0.5))[-1])
            wv("vector", sS, s3)
            def f(e, l=l, h=h, segc=segc):
                gns = GNC[:, l * H + h:l * H + h + 1]
                gnb = GNC[:, NL * H + l * H + h:NL * H + l * H + h + 1]
                e.tensor_mul(UB[:], UB[:], NMU[:])
                e.tensor_scalar(UB[:], UB[:], gns, gnb, ALU.mult, ALU.add)
                zt = GT[:, h * HALF + segc * 512:h * HALF + (segc + 1) * 512]
                return e.tensor_mul(zt, zt, UB[:])
            return op("vector", sV, f)

        for ui, (h, segc) in enumerate(units):
            eb = h % 2
            if segc == 0:
                wv("sync", sV, stats_read)
                e2d[h] = dma("sync", dW, E2S[:, eb], de2[h])
            sg = 2 * half + segc
            n_mt = 4 * sg + 4
            yb = 2 + (ui % 2)
            ypb = PS[yb]
            wv("tensor", sS, ydone[ui % 2])
            ylast = 0
            for mt in range(n_mt):
                ab = mt % 2
                wv("tensor", evs[ab], evd[ab])
                ta = op("tensor", sT, lambda e, h=h, mt=mt, ab=ab, segc=segc: e.matmul(
                    PS[ab][:], KT[:, h * S + mt * 128:h * S + (mt + 1) * 128],
                    QT[:, h * HALF + segc * 512:h * HALF + (segc + 1) * 512],
                    start=True, stop=True))
                c = mt - 4 * sg
                if c >= 0:
                    wv("vector", sT, ta)
                    ev = op("vector", sV, lambda e, ab=ab, c=c: e.tensor_tensor(
                        ATS[:, ab], PS[ab][:], BM[:, 384 - 128 * c:896 - 128 * c], ALU.mult))
                    evs[ab], evd[ab] = sV, ev
                else:
                    wv("scalar", sT, ta)
                    ev = op("scalar", sS, lambda e, ab=ab: e.activation(
                        ATS[:, ab], PS[ab][:], AF.Copy))
                    evs[ab], evd[ab] = sS, ev
                wv("tensor", evs[ab], evd[ab])
                ylast = op("tensor", sT, lambda e, h=h, mt=mt, ab=ab, yb=yb, n_mt=n_mt: e.matmul(
                    PS[yb][:], VB[:, mt * 1024 + h * 128:mt * 1024 + (h + 1) * 128],
                    ATS[:, ab], start=(mt == 0), stop=(mt == n_mt - 1)))
            if gn_pend is not None:
                gn_tail(gn_pend, ui - 1)
            gn_pend = (h, segc, ypb, ylast)
        gn_tail(gn_pend, len(units) - 1)
        barrier()

    def emit_out_half(l, half):
        for gj in range(2):
            for c in range(8):
                wb = c % 2
                join((sT,), ("sync",))
                dma("sync", dW, WMV[:, wb], wsp_att[l, 4, c * 128:(c + 1) * 128, :])
                join((dW,), ("tensor",))
                def f(e, c=c, wb=wb, gj=gj):
                    ins = None
                    for j4 in range(4):
                        jl = gj * 4 + j4
                        zs = GT[:, c * HALF + jl * 128:c * HALF + (jl + 1) * 128]
                        e.matmul(PS[j4 * 2][:], zs, WMV[:, wb][:, 0:512], start=(c == 0), stop=(c == 7))
                        ins = e.matmul(PS[j4 * 2 + 1][:], zs, WMV[:, wb][:, 512:1024],
                                       start=(c == 0), stop=(c == 7))
                    return ins
                op("tensor", sT, f)
            join((sT,), ("vector",))
            def f(e, gj=gj, half=half):
                ins = None
                for j4 in range(4):
                    nt = half * 8 + gj * 4 + j4
                    rt = r_tile(nt)
                    e.tensor_add(rt[:, 0:512], PS[j4 * 2][:], rt[:, 0:512])
                    ins = e.tensor_add(rt[:, 512:1024], PS[j4 * 2 + 1][:], rt[:, 512:1024])
                return ins
            op("vector", sV, f)
            join((sV,), ("tensor",))

    def emit_ffn_inner_half(l, half):
        tiles = list(range(half * 8, half * 8 + 8))
        for nt in tiles:
            ln_stats_tile(r_tile(nt), nt)
        ln_finish(tiles)
        for j in range(8):
            nt = half * 8 + j
            if j > 0:
                join((dG,), ("scalar",))
            ln_affine(r_tile(nt), ZROLL[:], nt)
            join((sS,), ("scalar",))
            for c in range(8):
                dma("scalar", dG, X2NT[:, c * HALF + j * 128:c * HALF + (j + 1) * 128],
                    ZROLL[:, c * 128:(c + 1) * 128], transpose=True)
        join((dG,), ("tensor",))
        for fc in range(16):
            for c in range(8):
                wb = c % 2
                join((sT,), ("sync",))
                dma("sync", dW, WMV[:, wb][:, 0:128],
                    wsp_w1[l, c * 128:(c + 1) * 128, fc * 128:(fc + 1) * 128])
                join((dW,), ("tensor",))
                def f(e, c=c, wb=wb):
                    e.matmul(PS[0][:], WMV[:, wb][:, 0:128], XNT[:, c * HALF:c * HALF + 512],
                             start=(c == 0), stop=(c == 7))
                    return e.matmul(PS[1][:], WMV[:, wb][:, 0:128],
                                    XNT[:, c * HALF + 512:c * HALF + 1024],
                                    start=(c == 0), stop=(c == 7))
                op("tensor", sT, f)
            join((sT,), ("scalar",))
            def f(e, fc=fc, l=l):
                e.activation(H1T[:, fc * HALF:fc * HALF + 512], PS[0][:],
                             AF.Gelu_apprx_tanh, bias=RW1CL[:, l * 16 + fc:l * 16 + fc + 1])
                return e.activation(H1T[:, fc * HALF + 512:(fc + 1) * HALF], PS[1][:],
                                    AF.Gelu_apprx_tanh, bias=RW1CL[:, l * 16 + fc:l * 16 + fc + 1])
            op("scalar", sS, f)
            join((sS,), ("tensor",))
        for gj in range(2):
            for fc in range(16):
                wb = fc % 2
                join((sT,), ("sync",))
                dma("sync", dW, WMV[:, wb], wsp_w2[l, fc * 128:(fc + 1) * 128, :])
                join((dW,), ("tensor",))
                def f(e, fc=fc, wb=wb, gj=gj):
                    ins = None
                    for j4 in range(4):
                        jl = gj * 4 + j4
                        hs = H1T[:, fc * HALF + jl * 128:fc * HALF + (jl + 1) * 128]
                        e.matmul(PS[j4 * 2][:], hs, WMV[:, wb][:, 0:512],
                                 start=(fc == 0), stop=(fc == 15))
                        ins = e.matmul(PS[j4 * 2 + 1][:], hs, WMV[:, wb][:, 512:1024],
                                       start=(fc == 0), stop=(fc == 15))
                    return ins
                op("tensor", sT, f)
            join((sT,), ("vector",))
            def f(e, gj=gj, half=half):
                ins = None
                for j4 in range(4):
                    nt = half * 8 + gj * 4 + j4
                    rt = r_tile(nt)
                    e.tensor_add(rt[:, 0:512], PS[j4 * 2][:], rt[:, 0:512])
                    ins = e.tensor_add(rt[:, 512:1024], PS[j4 * 2 + 1][:], rt[:, 512:1024])
                return ins
            op("vector", sV, f)
            join((sV,), ("tensor",))

    def emit_layer(l, b):
        if b > 0:
            dma("sync", dW, RVR1[0:1, :], rv_dram[l:l + 1, :])
        for nt in range(NT):
            ln_stats_tile(r_tile(nt), nt)
        ln_finish(list(range(NT)))
        emit_xnt_half(0)
        join((dG,), ("tensor",))
        emit_k_half(l, 0)
        emit_v_half(l, 0)
        emit_q_half(l, 0)
        emit_g_half(l, 0)
        emit_xnt_half(1)
        join((dG,), ("tensor",))
        emit_k_half(l, 1)
        emit_v_half(l, 1)
        barrier()
        emit_retention_half(l, 0)
        emit_out_half(l, 0)
        emit_ffn_inner_half(l, 0)
        barrier()
        emit_xnt_half(1)
        join((dG,), ("tensor",))
        emit_q_half(l, 1)
        emit_g_half(l, 1)
        barrier()
        tap("qth1_l%d" % l, QT)
        tap("gth1_l%d" % l, GT)
        emit_retention_half(l, 1)
        barrier()
        tap("zth1_l%d" % l, GT)
        emit_out_half(l, 1)
        barrier()
        tap("yr_l%d" % l, R[:])
        emit_ffn_inner_half(l, 1)
        barrier()

    def emit_quant(src_ap, w_, dst_ap):
        nb_ = w_ // 16
        def f(e, src_ap=src_ap, w_=w_, nb_=nb_):
            gv = src_ap.rearrange("p (g k) -> p g k", k=16)
            e.tensor_reduce(QM[:, 0:nb_], gv, AX.X, ALU.max, apply_absolute_value=True)
            e.drain()
            e.tensor_scalar(QSH[:, 0:nb_], QM[:, 0:nb_], 1.0 / 12.0, 0.5e-12, ALU.mult, ALU.max)
            e.drain()
            e.tensor_scalar(QSC[:, 0:nb_], QSH[:, 0:nb_], 2.0, None, ALU.mult)
            e.drain()
            e.reciprocal(QU[:, 0:nb_], QSC[:, 0:nb_])
            e.drain()
            ub = QU[:, 0:nb_].rearrange("p (g u) -> p g u", u=1).to_broadcast([128, nb_, 16])
            return e.tensor_tensor(QY[:, 0:w_].rearrange("p (g k) -> p g k", k=16), gv, ub, ALU.mult)
        op("vector", sV, f)
        join((sV,), ("scalar",))
        def f(e, w_=w_):
            e.activation(QSG[:, 0:w_], QY[:, 0:w_], AF.Sign)
            return e.activation(QY[:, 0:w_], QY[:, 0:w_], AF.Abs)
        op("scalar", sS, f)
        join((sS,), ("vector",))
        def f(e, w_=w_, nb_=nb_, dst_ap=dst_ap):
            a = QY[:, 0:w_]
            e.tensor_single_scalar(QT1[:, 0:w_], a, 0.25, ALU.is_gt)
            e.scalar_tensor_tensor(QT1[:, 0:w_], a, 0.75, QT1[:, 0:w_], ALU.is_gt, ALU.add)
            e.scalar_tensor_tensor(QT1[:, 0:w_], a, 1.25, QT1[:, 0:w_], ALU.is_gt, ALU.add)
            e.scalar_tensor_tensor(QT1[:, 0:w_], a, 1.75, QT1[:, 0:w_], ALU.is_gt, ALU.add)
            e.tensor_single_scalar(QT2[:, 0:w_], a, 2.5, ALU.is_gt)
            e.scalar_tensor_tensor(QT2[:, 0:w_], a, 3.5, QT2[:, 0:w_], ALU.is_gt, ALU.add)
            e.scalar_tensor_tensor(QT1[:, 0:w_], QT2[:, 0:w_], 2.0, QT1[:, 0:w_], ALU.mult, ALU.add)
            e.tensor_scalar(QT2[:, 0:w_], a, 5.0, 4.0, ALU.is_gt, ALU.mult)
            e.tensor_add(QT1[:, 0:w_], QT1[:, 0:w_], QT2[:, 0:w_])
            e.tensor_mul(QT1[:, 0:w_], QT1[:, 0:w_], QSG[:, 0:w_])
            shb = QSH[:, 0:nb_].rearrange("p (g u) -> p g u", u=1).to_broadcast([128, nb_, 16])
            return e.tensor_tensor(dst_ap.rearrange("p (g k) -> p g k", k=16),
                                   QT1[:, 0:w_].rearrange("p (g k) -> p g k", k=16), shb, ALU.mult)
        op("vector", sV, f)

    def emit_quant_wide(src_ap_fn, w_total, dst_ap_fn):
        off = 0
        while off < w_total:
            cw = min(QCW, w_total - off)
            emit_quant(src_ap_fn(off, cw), cw, dst_ap_fn(off, cw))
            off += cw

    def emit_ffn_wprep():
        for wi, src in enumerate((dfg, dfu)):
            for r in range(22):
                buf = r % 2
                join((sV,), ("sync",))
                dma("sync", dW, FQR[:, buf], src[r * 128:(r + 1) * 128, :])
                join((dW,), ("vector",))
                emit_quant_wide(lambda o, cw, buf=buf: FQR[:, buf][:, o:o + cw], D,
                                lambda o, cw: QQ[:, o:o + cw])
                join((sV,), ("scalar",))
                for c in range(8):
                    dma("scalar", dG, TRT[:, c * 128:(c + 1) * 128],
                        QQ[:, c * 128:(c + 1) * 128], transpose=True)
                join((dG,), ("gpsimd",))
                for c in range(8):
                    dma("gpsimd", dG, wsp_fgu[wi, :, c * FO + r * 128:c * FO + (r + 1) * 128],
                        TRT[:, c * 128:(c + 1) * 128])
                join((dG,), ("vector",))
        for r in range(8):
            for cc, wdt in ((0, 1024), (1024, 1024), (2048, 768)):
                buf = (cc // 1024) % 2
                join((sV,), ("sync",))
                dma("sync", dW, FQR[:, buf][:, 0:wdt], dfd[r * 128:(r + 1) * 128, cc:cc + wdt])
                join((dW,), ("vector",))
                emit_quant_wide(lambda o, cw, buf=buf: FQR[:, buf][:, o:o + cw], wdt,
                                lambda o, cw, cc=cc: QQ[:, cc + o:cc + o + cw])
            join((sV,), ("scalar",))
            for c in range(22):
                dma("scalar", dG, TRT[:, (c % 8) * 128:(c % 8 + 1) * 128],
                    QQ[:, c * 128:(c + 1) * 128], transpose=True)
                join((dG,), ("gpsimd",))
                dma("gpsimd", dG, wsp_fd[:, c * D + r * 128:c * D + (r + 1) * 128],
                    TRT[:, (c % 8) * 128:(c % 8 + 1) * 128])
            join((dG,), ("vector",))
        barrier()

    def emit_ffn_outer(b):
        for nt in range(NT):
            buf = nt % 2
            join((sS, sV, dG), ("sync",))
            dma("sync", dW, XT32[:, buf], dx[b, nt * 128:(nt + 1) * 128, :])
            join((dW,), ("vector",))
            op("vector", sV, lambda e, nt=nt, buf=buf: e.tensor_add(
                XT32[:, buf], XT32[:, buf], r_tile(nt)))
            join((sV,), ("gpsimd",))
            dma("gpsimd", dG, x2_dram[nt * 128:(nt + 1) * 128, :], XT32[:, buf])
            ln_stats_tile(XT32[:, buf], nt)
            ln_finish([nt])
            join((dG,), ("scalar",))
            ln_affine(XT32[:, buf], XT32[:, buf], nt)
            join((sS,), ("vector",))
            emit_quant_wide(lambda o, cw, buf=buf: XT32[:, buf][:, o:o + cw], D,
                            lambda o, cw: QQ[:, o:o + cw])
            join((sV,), ("scalar",))
            join((dG,), ("scalar",))
            for c in range(8):
                dma("scalar", dG, XQT[:, c * S + nt * 128:c * S + (nt + 1) * 128],
                    QQ[:, c * 128:(c + 1) * 128], transpose=True)
            join((dG,), ("vector", "scalar"))
        tap("xqt_b%d" % b, XQT)
        fsegs = [(0, 512), (512, 512), (1024, 512), (1536, 512), (2048, 512), (2560, 256)]
        join((dG,), ("tensor",))
        for fi, (f0, fw) in enumerate(fsegs):
            wb_ = fi % 2
            join((sT,), ("sync",))
            for c in range(8):
                dma("sync", dW, FGW[wb_][:, c * 512:c * 512 + fw],
                    wsp_fgu[0, :, c * FO + f0:c * FO + f0 + fw])
                dma("sync", dW, FUW[wb_][:, c * 512:c * 512 + fw],
                    wsp_fgu[1, :, c * FO + f0:c * FO + f0 + fw])
            for nt in range(NT):
                join((dW,), ("tensor",))
                if nt > 0:
                    join((sS, sV), ("tensor",))
                def f(e, nt=nt, fw=fw, wb_=wb_):
                    ins = None
                    for c in range(8):
                        xs = XQT[:, c * S + nt * 128:c * S + (nt + 1) * 128]
                        e.matmul(PS[0][:, 0:fw], xs, FGW[wb_][:, c * 512:c * 512 + fw],
                                 start=(c == 0), stop=(c == 7))
                        ins = e.matmul(PS[1][:, 0:fw], xs, FUW[wb_][:, c * 512:c * 512 + fw],
                                       start=(c == 0), stop=(c == 7))
                    return ins
                op("tensor", sT, f)
                join((sT,), ("scalar",))
                op("scalar", sS, lambda e, fw=fw: e.activation(
                    FSIL[:, 0:fw], PS[0][:, 0:fw], AF.Silu))
                join((sS,), ("vector",))
                op("vector", sV, lambda e, fw=fw: e.tensor_tensor(
                    FH[:, 0:fw], PS[1][:, 0:fw], FSIL[:, 0:fw], ALU.mult))
                emit_quant_wide(lambda o, cw: FH[:, o:o + cw], fw,
                                lambda o, cw: QQ[:, o:o + cw])
                join((sV,), ("scalar",))
                if nt > 0:
                    join((dG,), ("scalar",))
                for c in range(fw // 128):
                    dma("scalar", dG, TRT[:, c * 128:(c + 1) * 128],
                        QQ[:, c * 128:(c + 1) * 128], transpose=True)
                join((dG,), ("gpsimd",))
                for c in range(fw // 128):
                    fc = (f0 + c * 128) // 128
                    dma("gpsimd", dG, hq_dram[:, fc * S + nt * 128:fc * S + (nt + 1) * 128],
                        TRT[:, c * 128:(c + 1) * 128])
                join((dG,), ("vector",))
        barrier()
        for dh2 in range(2):
            for c in range(22):
                dma("sync", dW, FDW[:, c * 512:(c + 1) * 512],
                    wsp_fd[:, c * D + dh2 * 512:c * D + dh2 * 512 + 512])
            barrier()
            if dh2 == 0:
                tap("fdw_b%d" % b, FDW)
            for nt in range(NT):
                hb = nt % 2
                join((sT, sV, dG), ("sync",))
                for c in range(22):
                    dma("sync", dW, FHQ[hb][:, c * 128:(c + 1) * 128],
                        hq_dram[:, c * S + nt * 128:c * S + (nt + 1) * 128])
                dma("sync", dW, XT32[:, hb][:, dh2 * 512:dh2 * 512 + 512],
                    x2_dram[nt * 128:(nt + 1) * 128, dh2 * 512:dh2 * 512 + 512])
                join((dW,), ("tensor",))
                def f(e, hb=hb):
                    ins = None
                    for c in range(22):
                        ins = e.matmul(PS[6][:], FHQ[hb][:, c * 128:(c + 1) * 128],
                                       FDW[:, c * 512:(c + 1) * 512],
                                       start=(c == 0), stop=(c == 21))
                    return ins
                op("tensor", sT, f)
                join((sT,), ("vector",))
                op("vector", sV, lambda e, hb=hb, dh2=dh2: e.tensor_add(
                    XT32[:, hb][:, dh2 * 512:dh2 * 512 + 512], PS[6][:],
                    XT32[:, hb][:, dh2 * 512:dh2 * 512 + 512]))
                join((sV,), ("tensor", "gpsimd"))
                dma("gpsimd", dG, dout[b, nt * 128:(nt + 1) * 128, dh2 * 512:dh2 * 512 + 512],
                    XT32[:, hb][:, dh2 * 512:dh2 * 512 + 512])
                join((dG,), ("sync",))
        barrier()

    for b in range(n_batches):
        emit_top_ln(b)
        barrier()
        tap("topln_b%d" % b, R[:])
        if do_attn:
            for l in range(n_layers):
                if b == 0:
                    emit_wprep_layer(l)
                emit_layer(l, b)
                tap("layer%d_b%d" % (l, b), R[:])
        tap("kt_b%d" % b, KT[:])
        tap("vb_b%d" % b, VB[:])
        tap("qt_b%d" % b, QT)
        tap("gt_b%d" % b, GT)
        if do_ffn:
            if b == 0:
                emit_ffn_wprep()
            emit_ffn_outer(b)
        elif True:
            for nt in range(NT):
                buf = nt % 2
                if nt >= 2:
                    join((dG,), ("vector",))
                op("vector", sV, lambda e, nt=nt, buf=buf: e.tensor_copy(
                    XT32[:, buf], r_tile(nt)))
                join((sV,), ("gpsimd",))
                dma("gpsimd", dG, dout[b, nt * 128:(nt + 1) * 128, :], XT32[:, buf])
                join((dG,), ("vector",))
            barrier()
    barrier()

    with nc.Block() as block:
        def runner(name):
            def f(eng):
                for fn in P.streams[name]:
                    fn(eng)
            return f
        block.sync(runner("sync"))
        block.tensor(runner("tensor"))
        block.vector(runner("vector"))
        block.scalar(runner("scalar"))
        block.gpsimd(runner("gpsimd"))

    ctx.close()
    return nc


_CFG = {}


def _prep_inputs(inputs):
    hc = host_consts()
    wq = np.asarray(inputs["L0_attn_Wq"])
    wk = np.asarray(inputs["L0_attn_Wk"])
    colperm = np.concatenate([h * 128 + PERM for h in range(H)])
    wq_p = np.ascontiguousarray(wq[:, :, colperm])
    wk_p = np.ascontiguousarray(wk[:, :, colperm])
    lng = np.ascontiguousarray(np.asarray(inputs["L0_attn_ln_scale"]).reshape(8, 128).T)
    lnb = np.ascontiguousarray(np.asarray(inputs["L0_attn_ln_bias"]).reshape(8, 128).T)
    gns = np.asarray(inputs["L0_attn_gn_scale"]).reshape(NL, H, 128)
    gnb = np.asarray(inputs["L0_attn_gn_bias"]).reshape(NL, H, 128)
    gn = np.concatenate([gns.reshape(NL * H, 128), gnb.reshape(NL * H, 128)], 0).T
    common = {
        "wq": wq_p.astype(np.float32),
        "wk": wk_p.astype(np.float32),
        "wv": np.ascontiguousarray(np.asarray(inputs["L0_attn_Wv"], dtype=np.float32)),
        "wg": np.ascontiguousarray(np.asarray(inputs["L0_attn_Wg"], dtype=np.float32)),
        "wo": np.ascontiguousarray(np.asarray(inputs["L0_attn_Wo"], dtype=np.float32)),
        "w1": np.ascontiguousarray(np.asarray(inputs["L0_attn_W1"], dtype=np.float32)),
        "w2": np.ascontiguousarray(np.asarray(inputs["L0_attn_W2"], dtype=np.float32)),
        "lng": np.ascontiguousarray(lng.astype(np.float32)),
        "lnb": np.ascontiguousarray(lnb.astype(np.float32)),
        "gn": np.ascontiguousarray(gn.astype(np.float32)),
        "wfg": np.ascontiguousarray(np.asarray(inputs["L0_ffn_w_gate"], dtype=np.float32)),
        "wfu": np.ascontiguousarray(np.asarray(inputs["L0_ffn_w_up"], dtype=np.float32)),
        "wfd": np.ascontiguousarray(np.asarray(inputs["L0_ffn_w_down"], dtype=np.float32)),
        "cq": hc["cq"], "sq": hc["sq"], "ckt": hc["ckt"], "skt": hc["skt"],
        "e2t": hc["e2t"], "bigmask": hc["bigmask"], "ones128": hc["ones128"],
        "onesrow": hc["onesrow"], "pswap": hc["pswap"],
    }
    x = np.asarray(inputs["L0_attn_X"], dtype=np.float32)
    in_maps = []
    for c in range(8):
        m = dict(common)
        m["x"] = np.ascontiguousarray(x[2 * c:2 * c + 2])
        in_maps.append(m)
    return in_maps, hc


def kernel(**inputs):
    in_maps, hc = _prep_inputs(inputs)
    cfg = dict(_CFG)
    cfg["qseg"] = hc["qseg"]
    nc = build_program(cfg)
    res = run_bass_kernel_spmd(nc, in_maps, core_ids=list(range(8)))
    out = np.concatenate([np.asarray(r["out"]) for r in res.results], 0)
    return out.reshape(B, S, D).astype(np.float32)
